# version 01
# baseline (speedup 1.0000x reference)
import contextlib
import numpy as np
import ml_dtypes
import concourse.bass as bass
import concourse.mybir as mybir
from concourse.bass_utils import run_bass_kernel_spmd

F32 = mybir.dt.float32
BF16 = mybir.dt.bfloat16
I16 = mybir.dt.int16
I8 = mybir.dt.int8
AF = mybir.ActivationFunctionType
ALU = mybir.AluOpType
AX = mybir.AxisListType

TB = 512
NTB = 8
TCORE = TB * NTB
TBH = TB + 2
D = 1024
ND = 8
HID = 4096
NJ = 32
NCORES = 8


class Sched:
    ENGS = ("sync", "tensor", "vector", "scalar", "gpsimd")
    OWN = {"cV": "vector", "cA": "scalar", "cP": "tensor", "cG": "gpsimd"}

    def __init__(self, nc, es):
        self.nc = nc
        self.ops = {e: [] for e in self.ENGS}
        self.sems = {}
        self.counts = {}
        self.semname = {}
        self.strict = False
        self.last_tiny = {e: False for e in self.ENGS}
        for name in ("qS", "qG", "qA", "cV", "cA", "cP", "cG", "cc", "bar"):
            self.sems[name] = es.enter_context(nc.semaphore(name))
            self.counts[name] = 0
            self.semname[self.sems[name]] = name

    def do(self, eng, fn, after=(), inc=None, incval=1, tiny=False, strict_this=False):
        pend = {}
        own = {"vector": "cV", "scalar": "cA", "tensor": "cP", "gpsimd": "cG"}.get(eng)
        if (self.strict or strict_this or self.last_tiny.get(eng)) and own is not None \
                and self.counts[own] > 0:
            pend[self.sems[own]] = self.counts[own]
        self.last_tiny[eng] = tiny
        for tok in after:
            if tok is None:
                continue
            sem, val = tok
            if self.OWN.get(self.semname[sem]) == eng:
                continue
            if val > pend.get(sem, 0):
                pend[sem] = val
        tok = None
        sem = None
        if inc is not None:
            self.counts[inc] += incval
            sem = self.sems[inc]
            tok = (sem, self.counts[inc])
        self.ops[eng].append((pend, fn, sem, incval))
        return tok

    def dma(self, eng, out, in_, after=(), slow=False):
        q = {"sync": "qS", "gpsimd": "qG", "scalar": "qA"}[eng]
        if slow:
            return self.do(eng, lambda e: e.dma_start(out=out, in_=in_,
                                                      allow_slow_non_contiguous=True),
                           after=after, inc=q, incval=16)
        return self.do(eng, lambda e: e.dma_start(out=out, in_=in_),
                       after=after, inc=q, incval=16)

    def barrier(self):
        self.counts["bar"] += len(self.ENGS)
        target = self.counts["bar"]
        bar = self.sems["bar"]
        dmaq = {"sync": "qS", "gpsimd": "qG", "scalar": "qA"}
        for e in self.ENGS:
            pend = {}
            if e in dmaq and self.counts[dmaq[e]] > 0:
                pend[self.sems[dmaq[e]]] = self.counts[dmaq[e]]
            self.ops[e].append((pend, "BARRIER", bar, target))

    def emit(self, block):
        def mk(engn):
            def body(eng):
                for pend, fn, sem, val in self.ops[engn]:
                    for s, v in pend.items():
                        eng.wait_ge(s, v)
                    if fn == "BARRIER":
                        eng.sem_inc(sem, 1)
                        eng.wait_ge(sem, val)
                    else:
                        ins = fn(eng)
                        if sem is not None:
                            ins.then_inc(sem, val)
            return body
        block.sync(mk("sync"))
        block.tensor(mk("tensor"))
        block.vector(mk("vector"))
        block.scalar(mk("scalar"))
        block.gpsimd(mk("gpsimd"))


def build(taps=()):
    nc = bass.Bass("TRN2", target_bir_lowering=False)
    es = contextlib.ExitStack()
    dr = {}

    def din(name, shape, dt=F32):
        dr[name] = nc.dram_tensor(name, shape, dt, kind="ExternalInput")
        return dr[name]

    def dint(name, shape, dt=F32):
        dr[name] = nc.dram_tensor(name, shape, dt)
        return dr[name]

    x_fm = din("x_fm", [D, TCORE + 2])
    x_tm = din("x_tm", [TCORE, D])
    wk = din("wk", [D, D]); wv = din("wv", [D, D]); wo = din("wo", [D, D])
    w1t = din("w1t", [D, HID]); w2t = din("w2t", [HID, D])
    din("bk", [D]); din("bv", [D])
    din("bo_row", [1, D])
    din("wdec", [D]); din("urow", [D])
    din("convw", [HID, 3]); din("convb", [HID])
    din("iden", [128, 128])
    din("sel4", [4, TB])
    din("masks", [128, NCORES])

    out_t = nc.dram_tensor("out", [TCORE, D], F32, kind="ExternalOutput")

    for nm, dt_ in (("wkh_s", BF16), ("wkl_s", BF16), ("wvh_s", BF16),
                    ("wvl_s", BF16), ("woh_s", BF16), ("wol_s", BF16)):
        dint(nm, [D, D], dt_)
    dint("w1q_s", [NJ, ND, 128, 128], I8)
    dint("w2q_s", [NJ, 128, D], I8)
    dint("kspill", [NTB, ND, 128, TB])
    dint("vspill", [NTB, ND, 128, TB])
    dint("qspill", [ND, 128, TCORE + 4], BF16)
    dint("gspill", [NTB, 8, 128])
    dint("scal_d", [1, 8])
    dint("acc_d", [128, 1])
    dint("agin", [128, 24])
    dint("agout", [NCORES, 128, 24])

    tap_t = {}
    def tap(name, shape, dt=F32):
        tap_t[name] = nc.dram_tensor(name, shape, dt, kind="ExternalOutput")
    if "kv" in taps:
        tap("tap_k", [NTB, ND, 128, TB]); tap("tap_v", [NTB, ND, 128, TB])
    if "q" in taps:
        tap("tap_q", [ND, 128, TCORE + 4], BF16)
    if "xn" in taps:
        tap("tap_xn", [NTB, ND, 128, TBH])
    if "summ" in taps:
        tap("tap_summ", [128, 24])
    if "state" in taps:
        tap("tap_state", [128, 24])
    if "o" in taps:
        tap("tap_o", [NTB, ND, 128, TB])
    if "h" in taps:
        tap("tap_h", [NTB, 128, NJ * TBH])
    if "qh" in taps:
        tap("tap_qh", [NTB, 128, NJ * TB], BF16)
    if "g" in taps:
        tap("tap_g", [NTB, 8, 128])
    if "dbg" in taps:
        tap("tap_dbg", [16, 128, 520])
    if "sx" in taps:
        tap("tap_sx", [NTB, 4, 128])
        tap("tap_isx", [NTB, 128, 4])

    sb = lambda n, s, d=F32: es.enter_context(nc.sbuf_tensor(n, s, d))
    iota = sb("iota", [128, TB])
    ident = sb("ident", [128, 128])
    sel = sb("sel", [4, TB])
    ones128 = sb("ones128", [128, 1])
    ones1 = sb("ones1", [1, 128])
    ones2 = sb("ones2", [2, 128])
    bo_sb = sb("bo_sb", [1, D])
    masks_sb = sb("masks_sb", [128, NCORES])
    wpk = sb("wpk", [128, ND]); nwpk = sb("nwpk", [128, ND]); upk = sb("upk", [128, ND])
    w511n = sb("w511n", [128, ND]); w512n = sb("w512n", [128, ND]); w4096n = sb("w4096n", [128, ND])
    bkpk = sb("bkpk", [128, ND]); bvpk = sb("bvpk", [128, ND])
    cw0 = sb("cw0", [128, NJ]); cw1 = sb("cw1", [128, NJ]); cw2 = sb("cw2", [128, NJ])
    cbpk = sb("cbpk", [128, NJ])
    sw1c = sb("sw1c", [128, 1])
    sw2c = sb("sw2c", [128, 1])
    sw1m = sb("sw1m", [128, 1])
    sw2i127 = sb("sw2i127", [128, 1])
    Mpk = sb("Mpk", [128, ND]); Apk = sb("Apk", [128, ND]); Bpk = sb("Bpk", [128, ND])
    Spp = sb("Spp", [128, ND]); Saa = sb("Saa", [128, ND]); Sbb = sb("Sbb", [128, ND])
    ppst = sb("ppst", [128, ND]); aast = sb("aast", [128, ND]); bbst = sb("bbst", [128, ND])
    f8a = sb("f8a", [128, ND]); f8b = sb("f8b", [128, ND]); f8c = sb("f8c", [128, ND])
    f8d = sb("f8d", [128, ND])
    agpack = sb("agpack", [128, 24])
    epsc = sb("epsc", [128, 1])
    agall = sb("agall", [128, NCORES * 24])

    S = Sched(nc, es)
    dbgstage = sb("dbgstage", [128, 520])
    dbg_state = {"n": 0, "prev": None}
    def snap(ap_, width, part=128):
        if "tap_dbg" not in tap_t or dbg_state["n"] >= 16:
            return
        idx = dbg_state["n"]; dbg_state["n"] += 1
        t_c = S.do("vector", lambda e: e.tensor_copy(dbgstage[0:part, 0:width], ap_),
                   after=[dbg_state["prev"]], inc="cV")
        dbg_state["prev"] = S.dma("sync", tap_t["tap_dbg"].ap()[idx, 0:part, 0:width],
                                  dbgstage[0:part, 0:width], after=[t_c], slow=True)

    psA = es.enter_context(nc.psum_tensor("psA", [128, 4096], F32))
    def PS(col, width):
        return psA[:, col:col + width]
    def PSr(prow, col, width):
        return psA[0:prow, col:col + width] if col == 0 else bass.AP(psA, col, [[4096, prow], [1, width]])

    pro = contextlib.ExitStack()
    psb = lambda n, s, d=F32: pro.enter_context(nc.sbuf_tensor(n, s, d))
    wsf = psb("wsf", [128, HID])
    wsh = psb("wsh", [128, HID], BF16)
    wsl = psb("wsl", [128, D], BF16)
    wsi16 = psb("wsi16", [128, HID], I16)
    wsi8 = psb("wsi8", [128, HID], I8)
    tot1 = psb("tot1", [128, 1])
    acc1 = psb("acc1", [128, 1])
    acc32 = psb("acc32", [128, 32])
    scal1 = psb("scal1", [1, 8])

    S.do("gpsimd", lambda e: e.iota(iota[:, :], [[1, TB]], channel_multiplier=0,
                                    allow_small_or_imprecise_dtypes=True), inc="cG")
    S.do("gpsimd", lambda e: e.memset(ones128[:, :], 1.0), inc="cG")
    S.do("gpsimd", lambda e: e.memset(ones1[:, :], 1.0), inc="cG")
    S.do("gpsimd", lambda e: e.memset(ones2[:, :], 1.0), inc="cG")
    S.do("gpsimd", lambda e: e.memset(epsc[:, :], 1e-5), inc="cG")
    S.do("gpsimd", lambda e: e.memset(Spp[:, :], -1e30), inc="cG")
    S.do("gpsimd", lambda e: e.memset(Saa[:, :], 0.0), inc="cG")
    t_gpinit = S.do("gpsimd", lambda e: e.memset(Sbb[:, :], 0.0), inc="cG")

    S.dma("sync", ident[:, :], dr["iden"].ap())
    S.dma("sync", sel[:, :], dr["sel4"].ap())
    S.dma("sync", bo_sb[:, :], dr["bo_row"].ap())
    S.dma("sync", masks_sb[:, :], dr["masks"].ap())
    def pk_ap(name, n=ND):
        return bass.AP(dr[name], 0, [[1, 128], [128, n]])
    S.dma("sync", wpk[:, :], pk_ap("wdec"), slow=True)
    S.dma("sync", upk[:, :], pk_ap("urow"), slow=True)
    S.dma("sync", bkpk[:, :], pk_ap("bk"), slow=True)
    S.dma("sync", bvpk[:, :], pk_ap("bv"), slow=True)
    for t, tile in ((0, cw0), (1, cw1), (2, cw2)):
        S.dma("sync", tile[:, :], bass.AP(dr["convw"], t, [[3, 128], [3 * 128, NJ]]), slow=True)
    t_cst = S.dma("sync", cbpk[:, :], pk_ap("convb", NJ), slow=True)

    S.do("vector", lambda e: e.tensor_scalar_mul(nwpk[:, :], wpk[:, :], -1.0),
         after=[t_cst], inc="cV")
    S.do("vector", lambda e: e.tensor_scalar_mul(w511n[:, :], wpk[:, :], -float(TB - 1)), inc="cV")
    S.do("vector", lambda e: e.tensor_scalar_mul(w512n[:, :], wpk[:, :], -float(TB)), inc="cV")
    t_pk = S.do("vector", lambda e: e.tensor_scalar_mul(w4096n[:, :], wpk[:, :], -float(TCORE)),
                inc="cV")

    prev_st = []
    for (w_in, hi_nm, lo_nm) in ((wk, "wkh_s", "wkl_s"), (wv, "wvh_s", "wvl_s"),
                                 (wo, "woh_s", "wol_s")):
        for kc in range(ND):
            sl = slice(kc * 128, (kc + 1) * 128)
            t_l = S.dma("sync", wsf[:, :D], w_in.ap()[sl, :], after=prev_st)
            t_h = S.do("vector", lambda e: e.tensor_copy(wsh[:, :D], wsf[:, :D]),
                       after=[t_l], inc="cV")
            t_lo = S.do("vector", lambda e: e.tensor_tensor(wsl[:, :D], wsf[:, :D],
                                                            wsh[:, :D], ALU.subtract), inc="cV")
            t1 = S.dma("sync", dr[hi_nm].ap()[sl, :], wsh[:, :D], after=[t_h])
            t2 = S.dma("sync", dr[lo_nm].ap()[sl, :], wsl[:, :D], after=[t_lo])
            prev_st = [t1, t2, t_lo]

    def mean_scale(w_ap, n_sl, width, base, first_after):
        S.strict = True
        t_a = None
        for i in range(n_sl):
            t_l = S.dma("sync", wsf[:, :width], w_ap[i * 128:(i + 1) * 128, :],
                        after=first_after if i == 0 else [t_a])
            t_r = S.do("vector", lambda e, width=width: e.reduce_sum(
                tot1[:, :], wsf[:, :width], AX.X, apply_absolute_value=True),
                after=[t_l], inc="cV", tiny=True)
            if i == 0:
                t_a = S.do("vector", lambda e: e.tensor_copy(acc1[:, :], tot1[:, :]), inc="cV")
            else:
                t_a = S.do("vector", lambda e: e.tensor_tensor(acc1[:, :], acc1[:, :],
                                                               tot1[:, :], ALU.add), inc="cV")
        S.strict = True
        t_mm = S.do("tensor", lambda e: e.matmul(PSr(1, 0, 1), ones128[:, :], acc1[:, :],
                                                 start=True, stop=True), after=[t_a], inc="cP")
        t_s1 = S.do("vector", lambda e, base=base: e.tensor_scalar(
            scal1[0:1, base:base + 1], PSr(1, 0, 1), 1.0 / (D * HID), 1e-5,
            ALU.mult, ALU.max), after=[t_mm], inc="cV")
        t_s2 = S.do("vector", lambda e, base=base: e.reciprocal(
            scal1[0:1, base + 1:base + 2], scal1[0:1, base:base + 1]), inc="cV")
        S.strict = False
        return t_s2

    t_sw1 = mean_scale(w1t.ap(), ND, HID, 0, prev_st)
    t_sw2 = mean_scale(w2t.ap(), NJ, D, 2, [t_sw1])
    S.strict = True
    t_s127 = S.do("vector", lambda e: e.tensor_scalar_mul(scal1[0:1, 4:5], scal1[0:1, 2:3],
                                                          1.0 / 127.0), after=[t_sw2], inc="cV")
    t_sb1 = S.dma("sync", dr["scal_d"].ap(), scal1[0:1, 0:8], after=[t_s127])
    t_sb2 = S.dma("sync", scal1[0:1, 0:8], dr["scal_d"].ap(), after=[t_sb1])
    t_bc = S.do("tensor", lambda e: e.matmul(PS(512, 8), ones1[:, :], scal1[0:1, 0:8],
                                             start=True, stop=True), after=[t_sb2], inc="cP")
    S.do("vector", lambda e: e.tensor_copy(sw1m[:, :], PS(512, 8)[:, 0:1]), after=[t_bc], inc="cV")
    S.do("vector", lambda e: e.tensor_copy(sw1c[:, :], PS(512, 8)[:, 1:2]), inc="cV")
    S.do("vector", lambda e: e.tensor_copy(sw2c[:, :], PS(512, 8)[:, 3:4]), inc="cV")
    t_cols = S.do("vector", lambda e: e.tensor_copy(sw2i127[:, :], PS(512, 8)[:, 4:5]), inc="cV")
    S.strict = False

    t_prev = [t_cols]
    for kc in range(ND):
        t_l = S.dma("sync", wsf[:, :HID], w1t.ap()[kc * 128:(kc + 1) * 128, :], after=t_prev)
        t_m = S.do("vector", lambda e: e.tensor_scalar(wsf[:, :HID], wsf[:, :HID],
                                                       sw1c[:, 0:1], None, ALU.mult),
                   after=[t_l], inc="cV")
        S.do("vector", lambda e: e.tensor_copy(wsi16[:, :HID], wsf[:, :HID]), inc="cV")
        S.do("vector", lambda e: e.tensor_copy(wsh[:, :HID], wsi16[:, :HID]), inc="cV")
        S.do("vector", lambda e: e.tensor_scalar(wsh[:, :HID], wsh[:, :HID], -1.0, 1.0,
                                                 ALU.max, ALU.min), inc="cV")
        t_i8 = S.do("vector", lambda e: e.tensor_copy(wsi8[:, :HID], wsh[:, :HID]), inc="cV")
        src_ap = bass.AP(wsi8, 0, [[HID, 128], [128, NJ], [1, 128]])
        dst_ap = bass.AP(dr["w1q_s"], kc * 128 * 128, [[128, 128], [ND * 128 * 128, NJ], [1, 128]])
        t_prev = [S.dma("sync", dst_ap, src_ap, after=[t_i8])]
    for j in range(NJ):
        t_l = S.dma("sync", wsf[:, :D], w2t.ap()[j * 128:(j + 1) * 128, :], after=t_prev)
        t_m = S.do("vector", lambda e: e.tensor_scalar(wsf[:, :D], wsf[:, :D],
                                                       sw2c[:, 0:1], None, ALU.mult),
                   after=[t_l], inc="cV")
        S.do("vector", lambda e: e.tensor_copy(wsi16[:, :D], wsf[:, :D]), inc="cV")
        S.do("vector", lambda e: e.tensor_copy(wsh[:, :D], wsi16[:, :D]), inc="cV")
        S.do("vector", lambda e: e.tensor_scalar(wsh[:, :D], wsh[:, :D], -1.0, 1.0,
                                                 ALU.max, ALU.min), inc="cV")
        t_i8 = S.do("vector", lambda e: e.tensor_copy(wsi8[:, :D], wsh[:, :D]), inc="cV")
        t_prev = [S.dma("sync", dr["w2q_s"].ap()[j, :, :], wsi8[:, :D], after=[t_i8])]

    S.barrier()
    pro.close()

    p1 = contextlib.ExitStack()
    p1sb = lambda n, s, d=F32: p1.enter_context(nc.sbuf_tensor(n, s, d))
    xf = [p1sb(f"xf{k}", [128, TBH]) for k in range(ND)]
    xh = [p1sb(f"xh{k}", [128, TBH], BF16) for k in range(ND)]
    xl = [p1sb(f"xl{k}", [128, TBH], BF16) for k in range(ND)]
    qbt = p1sb("qbt", [128, TBH], BF16)
    q16t = p1sb("q16t", [128, TBH], I16)
    qf32 = p1sb("qf32", [128, TBH])
    sqt = p1sb("sqt", [128, TBH])
    sqt2 = p1sb("sqt2", [128, TBH])
    rowm = p1sb("rowm", [1, TBH]); rowq = p1sb("rowq", [1, TBH])
    rowt = p1sb("rowt", [1, TBH])
    rowpk = p1sb("rowpk", [1, 2 * TBH])
    mxt = p1sb("mxt", [128, TBH])
    mnt = p1sb("mnt", [128, TBH])
    mxtm = p1sb("mxtm", [128, 4])
    scr4 = p1sb("scr4", [128, 4])
    gtm = p1sb("gtm", [128, 4])
    sxro = p1sb("sxro", [4, 128])
    grow = p1sb("grow", [4, 128])
    halo2 = p1sb("halo2", [2, 128])
    hv2 = p1sb("hv2", [2, 4])
    hdiag = p1sb("hdiag", [2, 2])
    tpsb = p1sb("tpsb", [128, TB])
    kvb = [p1sb(f"kvb{x}", [128, TB]) for x in range(2)]
    hsc = p1sb("hsc", [128, TB])
    negM = p1sb("negM", [128, 1])
    wkhb = [p1sb(f"wkhb{k}", [128, D], BF16) for k in range(ND)]
    wklb = [p1sb(f"wklb{k}", [128, D], BF16) for k in range(ND)]
    wvhb = [p1sb(f"wvhb{k}", [128, D], BF16) for k in range(ND)]
    wvlb = [p1sb(f"wvlb{k}", [128, D], BF16) for k in range(ND)]

    ST_X, ST_Q, ST_H = 0, 512, 1024
    HALO_IS, HALO_MIS, HALO_SX = 1088, 1090, 1092
    TP2 = 1152
    BC_IS, BC_MIS, BC_SX = 1536, 2048, 2560
    TPD = 3072
    KV0, KV1 = 3072, 3584

    t_wsl = []
    for k in range(ND):
        sl = slice(k * 128, (k + 1) * 128)
        t_wsl.append(S.dma("gpsimd", wkhb[k][:, :], dr["wkh_s"].ap()[sl, :]))
        t_wsl.append(S.dma("gpsimd", wklb[k][:, :], dr["wkl_s"].ap()[sl, :]))
        t_wsl.append(S.dma("gpsimd", wvhb[k][:, :], dr["wvh_s"].ap()[sl, :]))
        t_wsl.append(S.dma("gpsimd", wvlb[k][:, :], dr["wvl_s"].ap()[sl, :]))
    t_wready = t_wsl[-1]

    prevc = {}
    for tb in range(NTB):
        col0 = tb * TB
        tx = []
        for k in range(ND):
            tx.append(S.dma("sync", xf[k][:, :],
                            bass.AP(dr["x_fm"], k * 128 * (TCORE + 2) + col0,
                                    [[TCORE + 2, 128], [1, TBH]])))
        last_stats = []
        t_m4hist = [None, None]
        for k in range(ND):
            sq_k = sqt if k % 2 == 0 else sqt2
            t_s = S.do("scalar", lambda e, k=k, sq_k=sq_k: e.activation(
                sq_k[:, :], xf[k][:, :], AF.Square),
                after=[tx[k], prevc.get("sq_rd"), t_m4hist[k % 2]], inc="cA")
            S.do("tensor", lambda e, k=k: e.matmul(PSr(1, ST_X, 512), ones128[:, :],
                                                   xf[k][:, 1:513],
                                                   start=(k == 0), stop=(k == ND - 1)),
                 after=[tx[k]], inc="cP")
            S.do("tensor", lambda e, k=k: e.matmul(
                bass.AP(psA, ST_H, [[4096, 1], [1, 2]]), ones128[:, :],
                bass.AP(xf[k], 0, [[TBH, 128], [TB + 1, 2]]),
                start=(k == 0), stop=(k == ND - 1)), inc="cP")
            S.do("tensor", lambda e, k=k, sq_k=sq_k: e.matmul(
                PSr(1, ST_Q, 512), ones128[:, :], sq_k[:, 1:513],
                start=(k == 0), stop=(k == ND - 1)),
                 after=[t_s], inc="cP")
            t_m4 = S.do("tensor", lambda e, k=k, sq_k=sq_k: e.matmul(
                bass.AP(psA, 1536, [[4096, 1], [1, 2]]), ones128[:, :],
                bass.AP(sq_k, 0, [[TBH, 128], [TB + 1, 2]]),
                start=(k == 0), stop=(k == ND - 1)), inc="cP")
            t_m4hist[k % 2] = t_m4
            last_stats = [t_m4]
        S.do("vector", lambda e: e.tensor_scalar_mul(rowm[:, 1:513], PSr(1, ST_X, 512), 1.0 / D),
             after=last_stats + [prevc.get("rows_done")], inc="cV")
        S.do("vector", lambda e: e.tensor_scalar_mul(
            bass.AP(rowm, 0, [[TBH, 1], [TB + 1, 2]]),
            bass.AP(psA, ST_H, [[4096, 1], [1, 2]]), 1.0 / D), inc="cV", tiny=True)
        S.do("vector", lambda e: e.tensor_scalar_mul(rowq[:, 1:513], PSr(1, ST_Q, 512), 1.0 / D),
             inc="cV")
        S.do("vector", lambda e: e.tensor_scalar_mul(
            bass.AP(rowq, 0, [[TBH, 1], [TB + 1, 2]]),
            bass.AP(psA, 1536, [[4096, 1], [1, 2]]), 1.0 / D), inc="cV", tiny=True)
        S.do("vector", lambda e: e.tensor_tensor(rowt[:, :], rowm[:, :], rowm[:, :], ALU.mult),
             inc="cV")
        S.do("vector", lambda e: e.tensor_tensor(rowq[:, :], rowq[:, :], rowt[:, :], ALU.subtract),
             inc="cV")
        S.do("vector", lambda e: e.tensor_scalar(rowq[:, :], rowq[:, :], 1e-5, None, ALU.add),
             inc="cV")
        t_sd = S.do("scalar", lambda e: e.activation(rowt[:, :], rowq[:, :], AF.Sqrt),
                    after=[(S.sems["cV"], S.counts["cV"])], inc="cA")
        S.do("vector", lambda e: e.reciprocal(rowpk[:, 0:TBH], rowt[:, :]),
             after=[t_sd], inc="cV")
        for _nit in range(2):
            S.do("vector", lambda e: e.tensor_tensor(rowt[:, :], rowpk[:, 0:TBH],
                                                     rowpk[:, 0:TBH], ALU.mult), inc="cV")
            S.do("vector", lambda e: e.tensor_tensor(rowt[:, :], rowt[:, :], rowq[:, :],
                                                     ALU.mult), inc="cV")
            S.do("vector", lambda e: e.tensor_scalar(rowt[:, :], rowt[:, :], -0.5, 1.5,
                                                     ALU.mult, ALU.add), inc="cV")
            S.do("vector", lambda e: e.tensor_tensor(rowpk[:, 0:TBH], rowpk[:, 0:TBH],
                                                     rowt[:, :], ALU.mult), inc="cV")
        t_is = S.do("vector", lambda e: e.engine_nop(), inc="cV")
        t_mis = S.do("vector", lambda e: e.tensor_tensor(rowpk[:, TBH:2 * TBH], rowm[:, :],
                                                         rowpk[:, 0:TBH], ALU.mult), inc="cV")
        t_bb = []
        t_bb.append(S.do("tensor", lambda e: e.matmul(PS(BC_IS, 512), ones1[:, :],
                                                      rowpk[0:1, 1:513], start=True, stop=True),
                         after=[t_is, prevc.get("xn_done")], inc="cP"))
        t_bb.append(S.do("tensor", lambda e: e.matmul(
            PS(HALO_IS, 2), ones1[:, :],
            bass.AP(rowpk, 0, [[2 * TBH, 1], [TB + 1, 2]]), start=True, stop=True), inc="cP"))
        t_bb.append(S.do("tensor", lambda e: e.matmul(PS(BC_MIS, 512), ones1[:, :],
                                                      rowpk[0:1, TBH + 1:TBH + 513],
                                                      start=True, stop=True),
                         after=[t_mis], inc="cP"))
        t_bb.append(S.do("tensor", lambda e: e.matmul(
            PS(HALO_MIS, 2), ones1[:, :],
            bass.AP(rowpk, TBH, [[2 * TBH, 1], [TB + 1, 2]]), start=True, stop=True), inc="cP"))
        t_xn = []
        for k in range(ND):
            S.do("vector", lambda e, k=k: e.tensor_tensor(xf[k][:, 1:513], xf[k][:, 1:513],
                                                          PS(BC_IS, 512), ALU.mult),
                 after=t_bb, inc="cV")
            S.do("vector", lambda e, k=k: e.tensor_tensor(
                bass.AP(xf[k], 0, [[TBH, 128], [TB + 1, 2]]),
                bass.AP(xf[k], 0, [[TBH, 128], [TB + 1, 2]]), PS(HALO_IS, 2), ALU.mult),
                inc="cV", tiny=True)
            S.do("vector", lambda e, k=k: e.tensor_tensor(xf[k][:, 1:513], xf[k][:, 1:513],
                                                          PS(BC_MIS, 512), ALU.subtract), inc="cV")
            t_xn.append(S.do("vector", lambda e, k=k: e.tensor_tensor(
                bass.AP(xf[k], 0, [[TBH, 128], [TB + 1, 2]]),
                bass.AP(xf[k], 0, [[TBH, 128], [TB + 1, 2]]), PS(HALO_MIS, 2), ALU.subtract),
                inc="cV", tiny=True))
        t_xn_done = t_xn[-1]
        if "tap_xn" in tap_t:
            for k in range(ND):
                S.dma("sync", tap_t["tap_xn"].ap()[tb, k, :, :], xf[k][:, :],
                      after=[t_xn[k]])
        t_lb = []
        for k in range(ND):
            S.do("vector", lambda e, k=k: e.tensor_copy(xh[k][:, :], xf[k][:, :]),
                 after=[t_xn[k], prevc.get("kv_done")], inc="cV")
            t_lb.append(S.do("vector", lambda e, k=k: e.tensor_tensor(
                xl[k][:, :], xf[k][:, :], xh[k][:, :], ALU.subtract), inc="cV"))
        for tile_, op_ in ((mxt, ALU.max), (mnt, ALU.min)):
            S.do("vector", lambda e, tile_=tile_, op_=op_: e.tensor_tensor(
                tile_[:, :], xf[0][:, :], xf[1][:, :], op_), inc="cV")
            for kk_ in range(2, ND):
                S.do("vector", lambda e, tile_=tile_, op_=op_, kk_=kk_: e.tensor_tensor(
                    tile_[:, :], tile_[:, :], xf[kk_][:, :], op_), inc="cV")
        t_mx = S.do("vector", lambda e: e.engine_nop(), inc="cV")
        t_tr = []
        for src_t, r in [(mxt, r) for r in range(4)]:
            t_tr.append(S.do("tensor", lambda e, r=r: e.transpose(
                PS(TPD + r * 128, 128), mxt[:, 1 + r * 128:1 + (r + 1) * 128], ident[:, :]),
                after=[t_mx, prevc.get("tp_free")], inc="cP"))
        S.strict = True
        t_ev0 = S.do("vector", lambda e: e.tensor_copy(tpsb[:, :], PS(TPD, 512)),
                     after=t_tr, inc="cV")
        S.do("vector", lambda e: e.tensor_reduce(
            mxtm[:, :], bass.AP(tpsb, 0, [[TB, 128], [128, 4], [1, 128]]), AX.X, ALU.max),
            inc="cV")
        t_tr = []
        for r in range(4):
            t_tr.append(S.do("tensor", lambda e, r=r: e.transpose(
                PS(TPD + r * 128, 128), mnt[:, 1 + r * 128:1 + (r + 1) * 128], ident[:, :]),
                after=[t_ev0, dbg_state["prev"]], inc="cP"))
        t_ev = S.do("vector", lambda e: e.tensor_copy(tpsb[:, :], PS(TPD, 512)),
                    after=t_tr, inc="cV")
        S.do("vector", lambda e: e.tensor_reduce(
            gtm[:, :], bass.AP(tpsb, 0, [[TB, 128], [128, 4], [1, 128]]), AX.X, ALU.min),
            inc="cV")
        S.do("vector", lambda e: e.tensor_scalar_mul(gtm[:, :], gtm[:, :], -1.0), inc="cV")
        t_rd = S.do("vector", lambda e: e.tensor_tensor(mxtm[:, :], mxtm[:, :], gtm[:, :],
                                                        ALU.max), inc="cV")
        S.do("vector", lambda e: e.tensor_scalar(mxtm[:, :], mxtm[:, :], 1e-5, 1.0 / 127.0,
                                                 ALU.max, ALU.mult), inc="cV")
        S.do("vector", lambda e: e.tensor_scalar(gtm[:, :], mxtm[:, :], sw1m[:, 0:1], None,
                                                 ALU.mult), inc="cV")
        S.do("vector", lambda e: e.reciprocal(scr4[:, :], mxtm[:, :]), inc="cV")
        t_sx = S.do("vector", lambda e: e.tensor_copy(mxtm[:, :], scr4[:, :]), inc="cV")
        t_tr2 = S.do("tensor", lambda e: e.transpose(PSr(4, TP2, 128), mxtm[:, :], ident[:, :]),
                     after=[t_sx, dbg_state["prev"]], inc="cP")
        t_ro = S.do("vector", lambda e: e.tensor_copy(sxro[:, :], PSr(4, TP2, 128)),
                    after=[t_tr2], inc="cV")
        t_tr3 = S.do("tensor", lambda e: e.transpose(PSr(4, TP2, 128), gtm[:, :], ident[:, :]),
                     after=[t_ro], inc="cP")
        t_go = S.do("vector", lambda e: e.tensor_copy(grow[:, :], PSr(4, TP2, 128)),
                    after=[t_tr3], inc="cV")
        t_htr = S.do("tensor", lambda e: e.transpose(
            PSr(2, TP2, 128), bass.AP(mxt, 0, [[TBH, 128], [TB + 1, 2]]), ident[:, :]),
            after=[t_go], inc="cP")
        t_hev = S.do("vector", lambda e: e.tensor_copy(halo2[:, :], PSr(2, TP2, 128)),
                     after=[t_htr], inc="cV")
        t_hrd0 = S.do("vector", lambda e: e.tensor_reduce(hv2[:, 0:1], halo2[:, :], AX.X, ALU.max),
                      inc="cV")
        t_htr2 = S.do("tensor", lambda e: e.transpose(
            PSr(2, TP2, 128), bass.AP(mnt, 0, [[TBH, 128], [TB + 1, 2]]), ident[:, :]),
            after=[t_hrd0], inc="cP")
        t_hev2 = S.do("vector", lambda e: e.tensor_copy(halo2[:, :], PSr(2, TP2, 128)),
                      after=[t_htr2], inc="cV")
        S.do("vector", lambda e: e.tensor_reduce(hv2[:, 3:4], halo2[:, :], AX.X, ALU.min),
             inc="cV")
        S.do("vector", lambda e: e.tensor_scalar_mul(hv2[:, 3:4], hv2[:, 3:4], -1.0), inc="cV")
        t_hrd = S.do("vector", lambda e: e.tensor_tensor(hv2[:, 0:1], hv2[:, 0:1], hv2[:, 3:4],
                                                         ALU.max), inc="cV")
        S.do("vector", lambda e: e.tensor_scalar(hv2[:, 0:1], hv2[:, 0:1], 1e-5, 1.0 / 127.0,
                                                 ALU.max, ALU.mult), inc="cV")
        S.do("vector", lambda e: e.tensor_scalar(hv2[:, 2:3], hv2[:, 0:1], sw1m[0:2, 0:1], None,
                                                 ALU.mult), inc="cV")
        t_hsx = S.do("vector", lambda e: e.reciprocal(hv2[:, 1:2], hv2[:, 0:1]), inc="cV")
        t_dg = S.do("vector", lambda e: e.tensor_tensor(
            hdiag[:, :], ident[0:2, 0:2], bass.AP(hv2, 1, [[4, 2], [0, 2]]), ALU.mult), inc="cV")
        t_hbc = S.do("tensor", lambda e: e.matmul(PS(HALO_SX, 2), ones2[:, :], hdiag[:, :],
                                                  start=True, stop=True), after=[t_dg], inc="cP")
        if tb == 0:
            snap(xf[0][:, 506:514], 8)
            snap(mxt[:, 506:514], 8)
            snap(halo2[:, :], 128, part=2)
            snap(hv2[:, :], 4, part=2)
        t_sb = []
        for r in range(4):
            t_sb.append(S.do("tensor", lambda e, r=r: e.matmul(
                PS(BC_SX + r * 128, 128), sel[:, r * 128:(r + 1) * 128], sxro[:, :],
                start=True, stop=True), after=[t_ro], inc="cP"))
        if "tap_sx" in tap_t:
            S.dma("sync", tap_t["tap_sx"].ap()[tb], sxro[:, :], after=[t_ro])
            S.dma("sync", tap_t["tap_isx"].ap()[tb],
                  scr4[:, :] if False else mxtm[:, :], after=[t_sx])
        t_gs = S.dma("sync", dr["gspill"].ap()[tb, 0:4, :], grow[:, :], after=[t_go])
        t_gs2 = S.dma("sync", bass.AP(dr["gspill"], tb * 8 * 128 + 4 * 128, [[1, 2], [1, 1]]),
                      bass.AP(hv2, 2, [[4, 2], [1, 1]]), after=[t_dg], slow=True)
        S.strict = False
        t_qs = []
        for k in range(ND):
            S.do("vector", lambda e, k=k: e.tensor_tensor(qf32[:, 1:513], xf[k][:, 1:513],
                                                          PS(BC_SX, 512), ALU.mult),
                 after=t_sb + [t_hbc], inc="cV")
            S.do("vector", lambda e, k=k: e.tensor_tensor(
                bass.AP(qf32, 0, [[TBH, 128], [TB + 1, 2]]),
                bass.AP(xf[k], 0, [[TBH, 128], [TB + 1, 2]]), PS(HALO_SX, 2), ALU.mult),
                inc="cV", tiny=True)
            S.do("vector", lambda e: e.tensor_copy(q16t[:, :], qf32[:, :]), inc="cV")
            t_q3 = S.do("vector", lambda e: e.tensor_copy(qbt[:, :], q16t[:, :]),
                        after=[t_qs[-1] if t_qs else None], inc="cV")
            t_qs.append(S.dma("sync", bass.AP(dr["qspill"], k * 128 * (TCORE + 4) + col0,
                                              [[TCORE + 4, 128], [1, TBH]]), qbt[:, :],
                              after=[t_q3]))
        prevc["sq_rd"] = None
        prevc["rows_done"] = t_bb[-1]
        prevc["xn_done"] = t_qs[-1]
        t_spk = t_spv = None
        t_sum_prev = prevc.get("sum_prev")
        for c in range(ND):
            pc = slice(c * 128, (c + 1) * 128)
            mmk = mmv = None
            for k in range(ND):
                S.do("tensor", lambda e, k=k, pc=pc: e.matmul(
                    PS(KV0, 512), wkhb[k][:, pc], xh[k][:, 1:513],
                    start=(k == 0), stop=False),
                    after=[t_lb[k], t_wready, prevc.get("kv0_free"), t_ev], inc="cP")
                S.do("tensor", lambda e, k=k, pc=pc: e.matmul(
                    PS(KV0, 512), wkhb[k][:, pc], xl[k][:, 1:513], start=False, stop=False),
                    inc="cP")
                mmk = S.do("tensor", lambda e, k=k, pc=pc: e.matmul(
                    PS(KV0, 512), wklb[k][:, pc], xh[k][:, 1:513], start=False,
                    stop=(k == ND - 1)), inc="cP")
            for k in range(ND):
                S.do("tensor", lambda e, k=k, pc=pc: e.matmul(
                    PS(KV1, 512), wvhb[k][:, pc], xh[k][:, 1:513],
                    start=(k == 0), stop=False),
                    after=[prevc.get("kv1_free")], inc="cP")
                S.do("tensor", lambda e, k=k, pc=pc: e.matmul(
                    PS(KV1, 512), wvhb[k][:, pc], xl[k][:, 1:513], start=False, stop=False),
                    inc="cP")
                mmv = S.do("tensor", lambda e, k=k, pc=pc: e.matmul(
                    PS(KV1, 512), wvlb[k][:, pc], xh[k][:, 1:513], start=False,
                    stop=(k == ND - 1)), inc="cP")
            t_ek = S.do("scalar", lambda e, c=c: e.activation(
                kvb[0][:, :], PS(KV0, 512), AF.Identity, bias=bkpk[:, c:c + 1], scale=1.0),
                after=[mmk, t_spk, t_sum_prev], inc="cA")
            prevc["kv0_free"] = t_ek
            t_ev_ = S.do("scalar", lambda e, c=c: e.activation(
                kvb[1][:, :], PS(KV1, 512), AF.Identity, bias=bvpk[:, c:c + 1], scale=1.0),
                after=[mmv, t_spv, t_sum_prev], inc="cA")
            prevc["kv1_free"] = t_ev_
            t_spk = S.dma("sync", dr["kspill"].ap()[tb, c, :, :], kvb[0][:, :], after=[t_ek])
            t_spv = S.dma("sync", dr["vspill"].ap()[tb, c, :, :], kvb[1][:, :], after=[t_ev_])
            if "tap_k" in tap_t:
                t_spk = S.dma("sync", tap_t["tap_k"].ap()[tb, c, :, :], kvb[0][:, :], after=[t_ek])
                t_spv = S.dma("sync", tap_t["tap_v"].ap()[tb, c, :, :], kvb[1][:, :], after=[t_ev_])
            t_h = S.do("vector", lambda e, c=c: e.scalar_tensor_tensor(
                hsc[:, :], iota[:, :], wpk[:, c:c + 1], kvb[0][:, :], ALU.mult, ALU.add),
                after=[t_ek, prevc.get("hsc_free")], inc="cV")
            t_M = S.do("vector", lambda e, c=c: e.tensor_reduce(
                Mpk[:, c:c + 1], hsc[:, :], AX.X, ALU.max), inc="cV", tiny=True)
            S.strict = True
            t_nM = S.do("vector", lambda e, c=c: e.tensor_scalar_mul(
                negM[:, :], Mpk[:, c:c + 1], -1.0), after=[prevc.get("negM_free")], inc="cV")
            S.strict = False
            t_eh = S.do("scalar", lambda e, c=c: e.activation(
                hsc[:, :], hsc[:, :], AF.Exp, bias=negM[:, 0:1], scale=1.0,
                accum_out=Bpk[:, c:c + 1]), after=[t_nM], inc="cA")
            prevc["negM_free"] = t_eh
            t_sum = S.do("vector", lambda e, c=c: e.scalar_tensor_tensor(
                hsc[:, :], hsc[:, :], 1.0, kvb[1][:, :], ALU.mult, ALU.mult,
                accum_out=Apk[:, c:c + 1]), after=[t_eh, t_ev_], inc="cV")
            prevc["hsc_free"] = t_sum
            t_sum_prev = t_sum
        prevc["sum_prev"] = t_sum_prev
        prevc["kv_done"] = t_sum_prev
        prevc["tp_free"] = t_sum_prev
        S.strict = True
        S.do("vector", lambda e: e.tensor_tensor(Mpk[:, :], Mpk[:, :], w511n[:, :], ALU.add),
             after=[t_gpinit], inc="cV")
        S.do("vector", lambda e: e.tensor_tensor(f8a[:, :], Spp[:, :], w512n[:, :], ALU.add),
             inc="cV")
        S.do("vector", lambda e: e.tensor_tensor(Spp[:, :], f8a[:, :], Mpk[:, :], ALU.max), inc="cV")
        S.do("vector", lambda e: e.tensor_tensor(f8a[:, :], f8a[:, :], Spp[:, :], ALU.subtract),
             inc="cV")
        t_e1 = S.do("scalar", lambda e: e.activation(f8a[:, :], f8a[:, :], AF.Exp),
                    after=[(S.sems["cV"], S.counts["cV"])], inc="cA")
        S.do("vector", lambda e: e.tensor_tensor(f8b[:, :], Mpk[:, :], Spp[:, :], ALU.subtract),
             inc="cV")
        t_e2 = S.do("scalar", lambda e: e.activation(f8b[:, :], f8b[:, :], AF.Exp),
                    after=[(S.sems["cV"], S.counts["cV"])], inc="cA")
        S.do("vector", lambda e: e.tensor_tensor(Saa[:, :], Saa[:, :], f8a[:, :], ALU.mult),
             after=[t_e1, t_e2], inc="cV")
        S.do("vector", lambda e: e.tensor_tensor(Sbb[:, :], Sbb[:, :], f8a[:, :], ALU.mult), inc="cV")
        S.do("vector", lambda e: e.tensor_tensor(f8c[:, :], Apk[:, :], f8b[:, :], ALU.mult), inc="cV")
        S.do("vector", lambda e: e.tensor_tensor(Saa[:, :], Saa[:, :], f8c[:, :], ALU.add), inc="cV")
        S.do("vector", lambda e: e.tensor_tensor(f8c[:, :], Bpk[:, :], f8b[:, :], ALU.mult), inc="cV")
        t_fold = S.do("vector", lambda e: e.tensor_tensor(Sbb[:, :], Sbb[:, :], f8c[:, :], ALU.add),
                      inc="cV")
        prevc["sum_prev"] = t_fold
        S.strict = False
        S.barrier()

    if "tap_q" in tap_t:
        S.dma("sync", tap_t["tap_q"].ap(), dr["qspill"].ap(), after=[t_fold])
    S.barrier()
    p1.close()

    S.do("vector", lambda e: e.tensor_copy(agpack[:, 0:ND], Spp[:, :]), inc="cV")
    S.do("vector", lambda e: e.tensor_copy(agpack[:, ND:2 * ND], Saa[:, :]), inc="cV")
    t_pkd = S.do("vector", lambda e: e.tensor_copy(agpack[:, 2 * ND:3 * ND], Sbb[:, :]), inc="cV")
    if "tap_summ" in tap_t:
        S.dma("sync", tap_t["tap_summ"].ap(), agpack[:, :], after=[t_pkd])
    t_ag1 = S.dma("gpsimd", dr["agin"].ap(), agpack[:, :], after=[t_pkd])
    t_cc = S.do("gpsimd", lambda e: e.collective_compute(
        "AllGather", ALU.bypass, replica_groups=[list(range(NCORES))],
        ins=[dr["agin"].ap().opt()], outs=[dr["agout"].ap().opt()]),
        after=[t_ag1], inc="cc")
    t_ag2 = S.dma("gpsimd", agall[:, :],
                  bass.AP(dr["agout"], 0, [[24, 128], [128 * 24, NCORES], [1, 24]]),
                  after=[t_cc])
    S.do("gpsimd", lambda e: e.memset(Spp[:, :], -1e20), after=[t_pkd], inc="cG")
    S.do("gpsimd", lambda e: e.memset(Saa[:, :], 0.0), inc="cG")
    S.do("gpsimd", lambda e: e.memset(Sbb[:, :], 0.0), inc="cG")
    t_ri = S.do("gpsimd", lambda e: e.memset(f8d[:, :], -1e30), inc="cG")
    S.strict = True
    prev = [t_ag2, t_ri]
    for r in range(NCORES):
        off = r * 24
        Tp = agall[:, off:off + ND]
        Ta = agall[:, off + ND:off + 2 * ND]
        Tb_ = agall[:, off + 2 * ND:off + 3 * ND]
        m_col = masks_sb[:, r:r + 1]
        m_b = bass.AP(masks_sb, r, [[NCORES, 128], [0, ND]]).bitcast(mybir.dt.int32)
        S.do("vector", lambda e, Tp=Tp, m_b=m_b: e.select(
            f8b[:, :], m_b, Tp, f8d[:, :]), after=prev, inc="cV")
        S.do("vector", lambda e, m_col=m_col: e.scalar_tensor_tensor(
            f8a[:, :], w4096n[:, :], m_col, Spp[:, :], ALU.mult, ALU.add), inc="cV")
        S.do("vector", lambda e: e.tensor_tensor(Spp[:, :], f8a[:, :], f8b[:, :], ALU.max),
             inc="cV")
        S.do("vector", lambda e: e.tensor_tensor(f8a[:, :], f8a[:, :], Spp[:, :], ALU.subtract),
             inc="cV")
        t_x1 = S.do("scalar", lambda e: e.activation(f8a[:, :], f8a[:, :], AF.Exp),
                    after=[(S.sems["cV"], S.counts["cV"])], inc="cA")
        S.do("vector", lambda e: e.tensor_tensor(f8b[:, :], f8b[:, :], Spp[:, :], ALU.subtract),
             inc="cV")
        t_x2 = S.do("scalar", lambda e: e.activation(f8b[:, :], f8b[:, :], AF.Exp),
                    after=[(S.sems["cV"], S.counts["cV"])], inc="cA")
        S.do("vector", lambda e: e.tensor_tensor(Saa[:, :], Saa[:, :], f8a[:, :], ALU.mult),
             after=[t_x1, t_x2], inc="cV")
        S.do("vector", lambda e: e.tensor_tensor(Sbb[:, :], Sbb[:, :], f8a[:, :], ALU.mult), inc="cV")
        S.do("vector", lambda e, Ta=Ta: e.tensor_tensor(f8c[:, :], Ta, f8b[:, :], ALU.mult), inc="cV")
        S.do("vector", lambda e: e.tensor_tensor(Saa[:, :], Saa[:, :], f8c[:, :], ALU.add), inc="cV")
        S.do("vector", lambda e, Tb_=Tb_: e.tensor_tensor(f8c[:, :], Tb_, f8b[:, :], ALU.mult),
             inc="cV")
        t_last = S.do("vector", lambda e: e.tensor_tensor(Sbb[:, :], Sbb[:, :], f8c[:, :], ALU.add),
                      inc="cV")
        prev = [t_last]
    S.do("vector", lambda e: e.tensor_scalar_max(ppst[:, :], Spp[:, :], -30.0), inc="cV")
    S.do("vector", lambda e: e.tensor_copy(aast[:, :], Saa[:, :]), inc="cV")
    t_st0 = S.do("vector", lambda e: e.tensor_copy(bbst[:, :], Sbb[:, :]), inc="cV")
    S.strict = False
    if "tap_state" in tap_t:
        S.do("vector", lambda e: e.tensor_copy(agpack[:, 0:ND], ppst[:, :]), inc="cV")
        S.do("vector", lambda e: e.tensor_copy(agpack[:, ND:2 * ND], aast[:, :]), inc="cV")
        tt_ = S.do("vector", lambda e: e.tensor_copy(agpack[:, 2 * ND:3 * ND], bbst[:, :]), inc="cV")
        S.dma("sync", tap_t["tap_state"].ap(), agpack[:, :], after=[tt_])
    S.barrier()

    p2 = contextlib.ExitStack()
    p2sb = lambda n, s, d=F32: p2.enter_context(nc.sbuf_tensor(n, s, d))
    kbuf = [p2sb(f"kb{c}", [128, TB]) for c in range(ND)]
    vbuf = [p2sb(f"vb{c}", [128, TB]) for c in range(ND)]
    qfb = [p2sb(f"qf{k}", [128, TBH], BF16) for k in range(ND)]
    oh = [vbuf[c].ap().bitcast(BF16)[:, 0:TB] for c in range(ND)]
    olw = [vbuf[c].ap().bitcast(BF16)[:, TB:2 * TB] for c in range(ND)]
    pp3 = p2sb("pp3", [128, TB + 1])
    aa3 = p2sb("aa3", [128, TB + 1])
    bb3 = p2sb("bb3", [128, TB + 1])
    sc1 = p2sb("sc1", [128, TB]); sc2 = p2sb("sc2", [128, TB])
    sc3 = p2sb("sc3", [128, TB]); sc4 = p2sb("sc4", [128, TB]); sc5 = p2sb("sc5", [128, TB])
    hbig = p2sb("hbig", [128, NJ * TBH])
    q16f = p2sb("q16f", [128, TB], I16)
    qhb = p2sb("qhb", [128, NJ * TB], BF16)
    mxh = p2sb("mxh", [128, TB])
    mnh = p2sb("mnh", [128, TB])
    mxtm2 = p2sb("mxtm2", [128, 4])
    scr4b = p2sb("scr4b", [128, 4])
    ishc = p2sb("ishc", [128, 4])
    shro = p2sb("shro", [4, 128])
    grow2 = p2sb("grow2", [4, 128])
    ghv = p2sb("ghv", [2, 1])
    hdiag2 = p2sb("hdiag2", [2, 2])
    w1b = [p2sb(f"w1b{x}", [128, D], BF16) for x in range(2)]
    w2b = [p2sb(f"w2b{x}", [128, D], BF16) for x in range(2)]
    wohb = w1b
    wolb = w2b
    xres = p2sb("xres", [128, D])
    outsb = p2sb("outsb", [128, 4 * D])
    gb_sb = p2sb("gb_sb", [128, TB])
    ghb_sb = p2sb("ghb_sb", [128, 2])

    H0, H1, HALO_H, GB, GHB = 0, 512, 1024, 1536, 1100
    TPD2, TP22, BCSH = 3072, 1152, 2560

    for tb in range(NTB):
        col0 = tb * TB
        tk, tv, tq = [], [], []
        for c in range(ND):
            tk.append(S.dma("sync", kbuf[c][:, :], dr["kspill"].ap()[tb, c, :, :],
                            after=[prevc.get(f"kfree{c}")]))
            tv.append(S.dma("sync", vbuf[c][:, :], dr["vspill"].ap()[tb, c, :, :],
                            after=[prevc.get(f"vfree{c}")]))
        for k in range(ND):
            tq.append(S.dma("sync", qfb[k][:, :],
                            bass.AP(dr["qspill"], k * 128 * (TCORE + 4) + col0,
                                    [[TCORE + 4, 128], [1, TBH]]),
                            after=[prevc.get("qfree")]))
        t_gl = S.dma("sync", grow2[:, :], dr["gspill"].ap()[tb, 0:4, :],
                     after=[prevc.get("gfree")])
        t_gl2 = S.dma("sync", ghv[0:2, 0:1], bass.AP(dr["gspill"], tb * 8 * 128 + 4 * 128,
                                                 [[1, 2], [1, 1]]),
                      after=[prevc.get("gfree")], slow=True)
        t_wkv = []
        pv = prevc
        for c in range(ND):
            nw_b = bass.AP(nwpk, c, [[ND, 128], [0, TB]])
            t_c0 = S.do("vector", lambda e, c=c: e.tensor_copy(pp3[:, 0:1], ppst[:, c:c + 1]),
                        after=[tk[c], t_st0, pv.get("pp3_free")], inc="cV", tiny=True)
            t_s1 = S.do("vector", lambda e, c=c, nw_b=nw_b: e.tensor_tensor_scan(
                pp3[:, 1:TB + 1], nw_b, kbuf[c][:, :], ppst[:, c:c + 1], ALU.add, ALU.max),
                inc="cV")
            S.do("vector", lambda e, c=c: e.tensor_copy(ppst[:, c:c + 1], pp3[:, TB:TB + 1]),
                 inc="cV", tiny=True, strict_this=True)
            t_a1 = S.do("vector", lambda e, c=c: e.scalar_tensor_tensor(
                sc1[:, :], pp3[:, 0:TB], nwpk[:, c:c + 1], pp3[:, 1:TB + 1],
                ALU.add, ALU.subtract), after=[pv.get("sc1_free")], inc="cV")
            t_e1 = S.do("scalar", lambda e: e.activation(sc1[:, :], sc1[:, :], AF.Exp),
                        after=[t_a1], inc="cA")
            t_a2 = S.do("vector", lambda e, c=c: e.tensor_tensor(
                sc2[:, :], kbuf[c][:, :], pp3[:, 1:TB + 1], ALU.subtract),
                after=[pv.get("sc2_free")], inc="cV")
            t_e2 = S.do("scalar", lambda e: e.activation(sc2[:, :], sc2[:, :], AF.Exp),
                        after=[t_a2], inc="cA")
            t_bv_ = S.do("gpsimd", lambda e, c=c: e.tensor_tensor(
                sc3[:, :], sc2[:, :], vbuf[c][:, :], ALU.mult),
                after=[t_e2, tv[c], pv.get("sc3_free")], inc="cG")
            t_w1_ = S.do("vector", lambda e, c=c: e.scalar_tensor_tensor(
                sc4[:, :], kbuf[c][:, :], upk[:, c:c + 1], pp3[:, 0:TB], ALU.add, ALU.subtract),
                after=[t_e1, pv.get("sc4_free")], inc="cV")
            t_r = S.do("scalar", lambda e: e.activation(sc4[:, :], sc4[:, :], AF.Exp),
                       after=[t_w1_], inc="cA")
            t_ca = S.do("vector", lambda e, c=c: e.tensor_copy(aa3[:, 0:1], aast[:, c:c + 1]),
                        after=[pv.get("aa3_free")], inc="cV", tiny=True)
            t_sa = S.do("vector", lambda e, c=c: e.tensor_tensor_scan(
                aa3[:, 1:TB + 1], sc1[:, :], sc3[:, :], aast[:, c:c + 1], ALU.mult, ALU.add),
                after=[t_bv_, t_e1], inc="cV")
            S.do("vector", lambda e, c=c: e.tensor_copy(aast[:, c:c + 1], aa3[:, TB:TB + 1]),
                 inc="cV", tiny=True, strict_this=True)
            t_cb = S.do("vector", lambda e, c=c: e.tensor_copy(bb3[:, 0:1], bbst[:, c:c + 1]),
                        after=[pv.get("bb3_free")], inc="cV", tiny=True)
            t_sb_ = S.do("vector", lambda e, c=c: e.tensor_tensor_scan(
                bb3[:, 1:TB + 1], sc1[:, :], sc2[:, :], bbst[:, c:c + 1], ALU.mult, ALU.add),
                inc="cV")
            S.do("vector", lambda e, c=c: e.tensor_copy(bbst[:, c:c + 1], bb3[:, TB:TB + 1]),
                 inc="cV", tiny=True, strict_this=True)
            t_n1 = S.do("gpsimd", lambda e, c=c: e.tensor_tensor(
                sc3[:, :], sc4[:, :], vbuf[c][:, :], ALU.mult), after=[t_r, t_sa], inc="cG")
            pv_n2 = S.do("vector", lambda e: e.tensor_tensor(
                sc3[:, :], sc3[:, :], aa3[:, 0:TB], ALU.add), after=[t_n1], inc="cV")
            t_d1 = S.do("gpsimd", lambda e: e.tensor_tensor(
                sc5[:, :], bb3[:, 0:TB], sc4[:, :], ALU.add),
                after=[t_r, t_sb_, pv.get("sc5_free")], inc="cG")
            t_d2 = S.do("vector", lambda e: e.reciprocal(sc2[:, :], sc5[:, :]),
                        after=[t_d1], inc="cV")
            t_o = S.do("vector", lambda e, c=c: e.tensor_tensor(
                kbuf[c][:, :], sc3[:, :], sc2[:, :], ALU.mult), inc="cV")
            if "tap_o" in tap_t:
                S.dma("sync", tap_t["tap_o"].ap()[tb, c, :, :], kbuf[c][:, :], after=[t_o])
            t_oh = S.do("vector", lambda e, c=c: e.tensor_copy(oh[c], kbuf[c][:, :]),
                        inc="cV")
            t_ol = S.do("vector", lambda e, c=c: e.tensor_tensor(
                olw[c], kbuf[c][:, :], oh[c], ALU.subtract), inc="cV")
            t_wkv.append(t_ol)
            pv = {"pp3_free": t_w1_, "sc1_free": t_sb_, "sc2_free": t_bv_,
                  "sc3_free": pv_n2, "sc4_free": t_d1, "sc5_free": t_o,
                  "aa3_free": pv_n2, "bb3_free": t_d1}
        t_wkv_done = t_wkv[-1]
        t_gb = []
        for r in range(4):
            t_gb.append(S.do("tensor", lambda e, r=r: e.matmul(
                PS(GB + r * 128, 128), sel[:, r * 128:(r + 1) * 128], grow2[:, :],
                start=True, stop=True), after=[t_gl, prevc.get("gb_free")], inc="cP"))
        S.strict = True
        t_dg2 = S.do("vector", lambda e: e.tensor_tensor(
            hdiag2[:, :], ident[0:2, 0:2], bass.AP(ghv, 0, [[1, 2], [0, 2]]), ALU.mult),
            after=[t_gl2], inc="cV")
        if tb == 0:
            snap(ghv[:, :], 1, part=2)
        S.strict = False
        t_ghb0 = S.do("tensor", lambda e: e.matmul(PS(GHB, 2), ones2[:, :], hdiag2[:, :],
                                                   start=True, stop=True),
                      after=[t_dg2, prevc.get("gb_free")], inc="cP")
        t_gbe = S.do("vector", lambda e: e.tensor_copy(gb_sb[:, :], PS(GB, 512)),
                     after=[t_gb[-1]], inc="cV")
        t_ghb = S.do("vector", lambda e: e.tensor_copy(ghb_sb[:, :], PS(GHB, 2)),
                     after=[t_ghb0], inc="cV")
        t_hev = []
        mm_prev = None
        for j in range(NJ):
            wbuf = w1b[j % 2]
            t_wl = S.dma("gpsimd",
                         bass.AP(wbuf, 0, [[D, 128], [128, ND], [1, 128]]),
                         bass.AP(dr["w1q_s"], j * ND * 128 * 128,
                                 [[128, 128], [128 * 128, ND], [1, 128]]),
                         after=[mm_prev if mm_prev is not None else prevc.get("w1_free")])
            hps = PS(H0 if j % 2 == 0 else H1, 512)
            mm_last = None
            for k in range(ND):
                pc = slice(k * 128, (k + 1) * 128)
                S.do("tensor", lambda e, k=k, hps=hps, pc=pc, wbuf=wbuf: e.matmul(
                    hps, wbuf[:, pc], qfb[k][:, 1:513],
                    start=(k == 0), stop=(k == ND - 1)),
                    after=[t_wl, tq[k], t_hev[-2] if len(t_hev) >= 2 else None], inc="cP")
                mm_last = S.do("tensor", lambda e, k=k, j=j, pc=pc, wbuf=wbuf: e.matmul(
                    PS(HALO_H + 2 * j, 2), wbuf[:, pc],
                    bass.AP(qfb[k], 0, [[TBH, 128], [TB + 1, 2]]),
                    start=(k == 0), stop=(k == ND - 1)), inc="cP")
            mm_prev = mm_last
            if tb == 0 and j == 0:
                snap(gb_sb[:, :], TB)
                snap(ghb_sb[:, :], 2)
                snap(wbuf[:, 0:512], 512)
                snap(qfb[0][:, :], TBH)
                snap(hps, TB)
            t_e = S.do("vector", lambda e, j=j, hps=hps: e.tensor_tensor(
                bass.AP(hbig, j * TBH + 1, [[NJ * TBH, 128], [1, TB]]),
                hps, gb_sb[:, :], ALU.mult),
                after=[mm_last, t_gbe, prevc.get("h_free"),
                       dbg_state["prev"] if (tb == 0 and j == 0) else None],
                inc="cV")
            t_hev.append(t_e)
        ghb_b = bass.AP(ghb_sb, 0, [[2, 128], [0, NJ], [1, 2]])
        t_hh = S.do("vector", lambda e: e.tensor_tensor(
            bass.AP(hbig, 0, [[NJ * TBH, 128], [TBH, NJ], [TB + 1, 2]]),
            bass.AP(psA, HALO_H, [[4096, 128], [2, NJ], [1, 2]]), ghb_b, ALU.mult),
            after=[t_hev[-1], t_ghb], inc="cV")
        t_tap_h = None
        if "tap_h" in tap_t:
            t_tap_h = S.dma("sync", tap_t["tap_h"].ap()[tb].opt(), hbig[:, :], after=[t_hh])
        t_silu = []
        t_m = None
        for j in range(NJ):
            base = j * TBH
            hmid = bass.AP(hbig, base + 1, [[NJ * TBH, 128], [1, TB]])
            hlef = bass.AP(hbig, base + 0, [[NJ * TBH, 128], [1, TB]])
            hrig = bass.AP(hbig, base + 2, [[NJ * TBH, 128], [1, TB]])
            t_a = S.do("scalar", lambda e, j=j, hlef=hlef: e.activation(
                sc1[:, :], hlef, AF.Identity, bias=0.0, scale=cw0[:, j:j + 1]),
                after=[t_hh, t_wkv_done, t_tap_h], inc="cA")
            t_b = S.do("vector", lambda e, j=j, hmid=hmid: e.scalar_tensor_tensor(
                sc2[:, :], hmid, cw1[:, j:j + 1], sc1[:, :], ALU.mult, ALU.add),
                after=[t_a], inc="cV")
            t_c2 = S.do("vector", lambda e, j=j, hrig=hrig: e.scalar_tensor_tensor(
                sc1[:, :], hrig, cw2[:, j:j + 1], sc2[:, :], ALU.mult, ALU.add),
                after=[t_b], inc="cV")
            t_s = S.do("scalar", lambda e, j=j, hmid=hmid: e.activation(
                hmid, sc1[:, :], AF.Silu, bias=cbpk[:, j:j + 1], scale=1.0),
                after=[t_c2], inc="cA")
            if j == 0:
                S.do("vector", lambda e, hmid=hmid: e.tensor_copy(
                    mxh[:, :], hmid), after=[t_s], inc="cV")
                t_m = S.do("vector", lambda e, hmid=hmid: e.tensor_copy(
                    mnh[:, :], hmid), inc="cV")
            else:
                S.do("vector", lambda e, hmid=hmid: e.tensor_tensor(
                    mxh[:, :], mxh[:, :], hmid, ALU.max), after=[t_s], inc="cV")
                t_m = S.do("vector", lambda e, hmid=hmid: e.tensor_tensor(
                    mnh[:, :], mnh[:, :], hmid, ALU.min), inc="cV")
            t_silu.append(t_m)
        t_tr = []
        for r in range(4):
            t_tr.append(S.do("tensor", lambda e, r=r: e.transpose(
                PS(TPD2 + r * 128, 128), mxh[:, r * 128:(r + 1) * 128], ident[:, :]),
                after=[t_m], inc="cP"))
        S.strict = True
        t_ev2 = S.do("vector", lambda e: e.tensor_copy(sc3[:, :], PS(TPD2, 512)),
                     after=t_tr, inc="cV")
        S.do("vector", lambda e: e.tensor_reduce(
            mxtm2[:, :], bass.AP(sc3, 0, [[TB, 128], [128, 4], [1, 128]]), AX.X, ALU.max),
            inc="cV")
        t_tr = []
        for r in range(4):
            t_tr.append(S.do("tensor", lambda e, r=r: e.transpose(
                PS(TPD2 + r * 128, 128), mnh[:, r * 128:(r + 1) * 128], ident[:, :]),
                after=[t_ev2], inc="cP"))
        t_ev2b = S.do("vector", lambda e: e.tensor_copy(sc3[:, :], PS(TPD2, 512)),
                      after=t_tr, inc="cV")
        S.do("vector", lambda e: e.tensor_reduce(
            ishc[:, :], bass.AP(sc3, 0, [[TB, 128], [128, 4], [1, 128]]), AX.X, ALU.min),
            inc="cV")
        S.do("vector", lambda e: e.tensor_scalar_mul(ishc[:, :], ishc[:, :], -1.0), inc="cV")
        t_rd2 = S.do("vector", lambda e: e.tensor_tensor(mxtm2[:, :], mxtm2[:, :], ishc[:, :],
                                                         ALU.max), inc="cV")
        S.do("vector", lambda e: e.tensor_scalar_max(mxtm2[:, :], mxtm2[:, :], 1e-5), inc="cV")
        S.do("vector", lambda e: e.tensor_scalar(ishc[:, :], mxtm2[:, :], sw2i127[:, 0:1],
                                                 None, ALU.mult), inc="cV")
        S.do("vector", lambda e: e.tensor_scalar_mul(mxtm2[:, :], mxtm2[:, :], 1.0 / 127.0),
             inc="cV")
        t_sh2a = S.do("vector", lambda e: e.reciprocal(scr4b[:, :], mxtm2[:, :]), inc="cV")
        t_sh2 = S.do("vector", lambda e: e.tensor_copy(mxtm2[:, :], scr4b[:, :]), inc="cV")
        t_tr2 = S.do("tensor", lambda e: e.transpose(PSr(4, TP22, 128), mxtm2[:, :], ident[:, :]),
                     after=[t_sh2], inc="cP")
        t_ro2 = S.do("vector", lambda e: e.tensor_copy(shro[:, :], PSr(4, TP22, 128)),
                     after=[t_tr2], inc="cV")
        t_shb = []
        for r in range(4):
            t_shb.append(S.do("tensor", lambda e, r=r: e.matmul(
                PS(BCSH + r * 128, 128), sel[:, r * 128:(r + 1) * 128], shro[:, :],
                start=True, stop=True), after=[t_ro2], inc="cP"))
        S.strict = False
        t_qh = []
        for j in range(NJ):
            base = j * TBH
            hmid = bass.AP(hbig, base + 1, [[NJ * TBH, 128], [1, TB]])
            t_q1 = S.do("vector", lambda e, hmid=hmid: e.tensor_tensor(
                q16f[:, :], hmid, PS(BCSH, 512), ALU.mult),
                after=t_shb + [t_qh[-1] if t_qh else None], inc="cV")
            t_q2 = S.do("gpsimd", lambda e, j=j: e.tensor_copy(
                qhb[:, j * TB:(j + 1) * TB], q16f[:, :]), after=[t_q1], inc="cG")
            t_qh.append(t_q2)
        if "tap_qh" in tap_t:
            S.dma("sync", tap_t["tap_qh"].ap()[tb].opt(), qhb[:, :], after=[t_qh[-1]])
        t_wo_mm = []
        t_l1 = t_l2 = None
        for k in range(ND):
            t_l1 = S.dma("gpsimd", wohb[k % 2][:, :], dr["woh_s"].ap()[k * 128:(k + 1) * 128, :],
                         after=[t_wo_mm[-1] if t_wo_mm else prevc.get("wo_free"),
                                t_qh[-1] if k == 0 else None])
            t_l2 = S.dma("gpsimd", wolb[k % 2][:, :], dr["wol_s"].ap()[k * 128:(k + 1) * 128, :])
            for m in range(4):
                ms = slice(m * 128, (m + 1) * 128)
                for half in range(2):
                    cps = PS(m * 1024 + half * 512, 512)
                    hs = slice(half * 512, (half + 1) * 512)
                    S.do("tensor", lambda e, k=k, ms=ms, hs=hs, cps=cps: e.matmul(
                        cps, oh[k][:, ms], wohb[k % 2][:, hs],
                        start=(k == 0), stop=False),
                        after=[t_l1, t_l2, t_wkv[k], t_shb[-1] if k == 0 else None], inc="cP")
                    S.do("tensor", lambda e, k=k, ms=ms, hs=hs, cps=cps: e.matmul(
                        cps, oh[k][:, ms], wolb[k % 2][:, hs], start=False, stop=False), inc="cP")
                    t_wo_mm.append(S.do("tensor", lambda e, k=k, ms=ms, hs=hs, cps=cps: e.matmul(
                        cps, olw[k][:, ms], wohb[k % 2][:, hs], start=False, stop=False),
                        inc="cP"))
        t_bo_mm = []
        for m in range(4):
            for half in range(2):
                cps = PS(m * 1024 + half * 512, 512)
                t_bo_mm.append(S.do("tensor", lambda e, cps=cps, half=half: e.matmul(
                    cps, ones1[:, :], bo_sb[0:1, half * 512:(half + 1) * 512],
                    start=False, stop=True), inc="cP"))
        t_att = []
        for m in range(4):
            t_x = S.dma("sync", xres[:, :] if m == 0 else xres[:, :],
                        x_tm.ap()[col0 + m * 128: col0 + (m + 1) * 128, :],
                        after=[t_bo_mm[2 * m + 1], prevc.get("xres_free")])
            t_e = S.do("vector", lambda e, m=m: e.tensor_tensor(
                outsb[:, m * D:(m + 1) * D], bass.AP(psA, m * 1024, [[4096, 128], [1, 1024]]),
                xres[:, :], ALU.add), after=[t_x, t_bo_mm[2 * m + 1], prevc.get("out_free")],
                inc="cV")
            prevc["xres_free"] = t_e
            t_att.append(t_e)
        t_w2_mm = []
        t_wl2_prev = None
        for j in range(NJ):
            wbuf = w2b[j % 2]
            t_wl2 = S.dma("gpsimd", wbuf[:, :], dr["w2q_s"].ap()[j, :, :],
                          after=[t_w2_mm[-1] if t_w2_mm else t_att[-1]])
            for m in range(4):
                for half in range(2):
                    cps = PS(m * 1024 + half * 512, 512)
                    t_w2_mm.append(S.do("tensor", lambda e, j=j, m=m, half=half, wbuf=wbuf, cps=cps: e.matmul(
                        cps, qhb[:, j * TB + m * 128:j * TB + (m + 1) * 128],
                        wbuf[:, half * 512:(half + 1) * 512],
                        start=(j == 0), stop=(j == NJ - 1)),
                        after=[t_wl2, t_qh[j], t_att[-1] if j == 0 else None], inc="cP"))
        t_outs = []
        for m in range(4):
            for half in range(2):
                cps = PS(m * 1024 + half * 512, 512)
                t_f = S.do("vector", lambda e, m=m, half=half, cps=cps: e.scalar_tensor_tensor(
                    outsb[:, m * D + half * 512:m * D + (half + 1) * 512],
                    cps, ishc[:, m:m + 1],
                    outsb[:, m * D + half * 512:m * D + (half + 1) * 512],
                    ALU.mult, ALU.add),
                    after=[t_w2_mm[-(8 - 2 * m - half)] if False else t_w2_mm[-1]], inc="cV")
                t_outs.append(t_f)
            t_od = S.dma("sync", out_t.ap()[col0 + m * 128: col0 + (m + 1) * 128, :],
                         outsb[:, m * D:(m + 1) * D], after=[t_outs[-1]])
            prevc["out_free"] = t_od
        prevc["wo_free"] = t_outs[-1]
        prevc["w1_free"] = t_outs[-1]
        prevc["h_free"] = t_qh[-1]
        prevc["gb_free"] = t_hev[-1]
        prevc["gfree"] = t_ghb
        prevc["qfree"] = t_w2_mm[-1]
        for c in range(ND):
            prevc[f"kfree{c}"] = t_wo_mm[-1]
            prevc[f"vfree{c}"] = t_bo_mm[-1]
        S.barrier()
    p2.close()

    S.do("sync", lambda e: e.nop(), after=[(S.sems["qS"], S.counts["qS"])])

    with nc.Block() as block:
        S.emit(block)
    return nc, tap_t


def _host_prep(inputs):
    x = np.asarray(inputs["L0_attn_x"])
    maps = []
    base = dict(
        wk=np.ascontiguousarray(inputs["L0_attn_Wk"], dtype=np.float32),
        wv=np.ascontiguousarray(inputs["L0_attn_Wv"], dtype=np.float32),
        wo=np.ascontiguousarray(inputs["L0_attn_Wo"], dtype=np.float32),
        w1t=np.ascontiguousarray(np.asarray(inputs["L0_ffn_w1"]).T, dtype=np.float32),
        w2t=np.ascontiguousarray(np.asarray(inputs["L0_ffn_w2"]).T, dtype=np.float32),
        bk=np.ascontiguousarray(inputs["L0_attn_bk"], dtype=np.float32),
        bv=np.ascontiguousarray(inputs["L0_attn_bv"], dtype=np.float32),
        bo_row=np.ascontiguousarray(np.asarray(inputs["L0_attn_bo"], dtype=np.float32)[None, :]),
        wdec=np.ascontiguousarray(np.asarray(inputs["L0_attn_w_rwkv"], dtype=np.float32).reshape(-1)),
        urow=np.ascontiguousarray(np.asarray(inputs["L0_attn_u_rwkv"], dtype=np.float32).reshape(-1)),
        convw=np.ascontiguousarray(np.asarray(inputs["L0_ffn_conv_w"], dtype=np.float32).reshape(HID, 3)),
        convb=np.ascontiguousarray(inputs["L0_ffn_conv_b"], dtype=np.float32),
        iden=np.eye(128, dtype=np.float32),
    )
    sel4 = np.zeros((4, TB), dtype=np.float32)
    for r in range(4):
        sel4[r, r * 128:(r + 1) * 128] = 1.0
    base["sel4"] = sel4
    for c in range(NCORES):
        b, qtr = c // 4, c % 4
        t0 = qtr * TCORE
        xb = np.asarray(x[b], dtype=np.float32)
        fm = np.zeros((D, TCORE + 2), dtype=np.float32)
        lo, hi = t0 - 1, t0 + TCORE + 1
        s_lo, s_hi = max(lo, 0), min(hi, 16384)
        fm[:, s_lo - lo:s_hi - lo] = xb[s_lo:s_hi].T
        masks = np.zeros((128, NCORES), dtype=np.float32)
        for r in range(4 * b, 4 * b + qtr):
            masks[:, r] = 1.0
        m = dict(base)
        m["x_fm"] = np.ascontiguousarray(fm)
        m["x_tm"] = np.ascontiguousarray(xb[t0:t0 + TCORE])
        m["masks"] = masks
        maps.append(m)
    return maps


def kernel(**inputs):
    maps = _host_prep(inputs)
    nc, _ = build()
    res = run_bass_kernel_spmd(nc, maps, core_ids=list(range(NCORES)))
    out = np.empty((2, 16384, D), dtype=np.float32)
    for c in range(NCORES):
        b, qtr = c // 4, c % 4
        out[b, qtr * TCORE:(qtr + 1) * TCORE] = res.results[c]["out"]
    return out


# version 04
# speedup vs baseline: 1.5472x; 1.5472x over previous
import contextlib
import numpy as np
import ml_dtypes
import concourse.bass as bass
import concourse.mybir as mybir
from concourse.bass_utils import run_bass_kernel_spmd

F32 = mybir.dt.float32
BF16 = mybir.dt.bfloat16
I16 = mybir.dt.int16
I8 = mybir.dt.int8
AF = mybir.ActivationFunctionType
ALU = mybir.AluOpType
AX = mybir.AxisListType

TB = 512
INTERLEAVE = True
P2LVL = 6
NTB = 8
TCORE = TB * NTB
TBH = TB + 2
D = 1024
ND = 8
HID = 4096
NJ = 32
NCORES = 8


class Sched:
    ENGS = ("sync", "tensor", "vector", "scalar", "gpsimd")
    OWN = {"cV": "vector", "cA": "scalar", "cP": "tensor", "cG": "gpsimd"}

    def __init__(self, nc, es):
        self.nc = nc
        self.ops = {e: [] for e in self.ENGS}
        self.sems = {}
        self.counts = {}
        self.semname = {}
        self.strict = False
        self.last_tiny = {e: False for e in self.ENGS}
        for name in ("qS", "qG", "qA", "cV", "cA", "cP", "cG", "cc", "bar"):
            self.sems[name] = es.enter_context(nc.semaphore(name))
            self.counts[name] = 0
            self.semname[self.sems[name]] = name

    def do(self, eng, fn, after=(), inc=None, incval=1, tiny=False, strict_this=False):
        pend = {}
        own = {"vector": "cV", "scalar": "cA", "tensor": "cP", "gpsimd": "cG"}.get(eng)
        if (self.strict or strict_this or self.last_tiny.get(eng)) and own is not None \
                and self.counts[own] > 0:
            pend[self.sems[own]] = self.counts[own]
        self.last_tiny[eng] = tiny
        for tok in after:
            if tok is None:
                continue
            sem, val = tok
            if self.OWN.get(self.semname[sem]) == eng:
                continue
            if val > pend.get(sem, 0):
                pend[sem] = val
        tok = None
        sem = None
        if inc is not None:
            self.counts[inc] += incval
            sem = self.sems[inc]
            tok = (sem, self.counts[inc])
        self.ops[eng].append((pend, fn, sem, incval))
        return tok

    def dma(self, eng, out, in_, after=(), slow=False):
        q = {"sync": "qS", "gpsimd": "qG", "scalar": "qA"}[eng]
        if slow:
            return self.do(eng, lambda e: e.dma_start(out=out, in_=in_,
                                                      allow_slow_non_contiguous=True),
                           after=after, inc=q, incval=16)
        return self.do(eng, lambda e: e.dma_start(out=out, in_=in_),
                       after=after, inc=q, incval=16)

    def barrier(self):
        self.counts["bar"] += len(self.ENGS)
        target = self.counts["bar"]
        bar = self.sems["bar"]
        dmaq = {"sync": "qS", "gpsimd": "qG", "scalar": "qA"}
        for e in self.ENGS:
            pend = {}
            if e in dmaq and self.counts[dmaq[e]] > 0:
                pend[self.sems[dmaq[e]]] = self.counts[dmaq[e]]
            self.ops[e].append((pend, "BARRIER", bar, target))

    def emit(self, block):
        def mk(engn):
            def body(eng):
                for pend, fn, sem, val in self.ops[engn]:
                    for s, v in pend.items():
                        eng.wait_ge(s, v)
                    if fn == "BARRIER":
                        eng.sem_inc(sem, 1)
                        eng.wait_ge(sem, val)
                    else:
                        ins = fn(eng)
                        if sem is not None:
                            ins.then_inc(sem, val)
            return body
        block.sync(mk("sync"))
        block.tensor(mk("tensor"))
        block.vector(mk("vector"))
        block.scalar(mk("scalar"))
        block.gpsimd(mk("gpsimd"))


def build(taps=()):
    nc = bass.Bass("TRN2", target_bir_lowering=False)
    es = contextlib.ExitStack()
    dr = {}

    def din(name, shape, dt=F32):
        dr[name] = nc.dram_tensor(name, shape, dt, kind="ExternalInput")
        return dr[name]

    def dint(name, shape, dt=F32):
        dr[name] = nc.dram_tensor(name, shape, dt)
        return dr[name]

    x_fm = din("x_fm", [D, TCORE + 2])
    x_tm = din("x_tm", [TCORE, D])
    wk = din("wk", [D, D]); wv = din("wv", [D, D]); wo = din("wo", [D, D])
    w1t = din("w1t", [D, HID]); w2t = din("w2t", [HID, D])
    din("bk", [D]); din("bv", [D])
    din("bo_row", [1, D])
    din("wdec", [D]); din("urow", [D])
    din("convw", [HID, 3]); din("convb", [HID])
    din("iden", [128, 128])
    din("sel4", [4, TB])
    din("masks", [128, NCORES])

    out_t = nc.dram_tensor("out", [TCORE, D], F32, kind="ExternalOutput")

    for nm, dt_ in (("wkh_s", BF16), ("wkl_s", BF16), ("wvh_s", BF16),
                    ("wvl_s", BF16), ("woh_s", BF16), ("wol_s", BF16)):
        dint(nm, [D, D], dt_)
    dint("w1q_s", [NJ, ND, 128, 128], I8)
    dint("w2q_s", [NJ, 128, D], I8)
    dint("kspill", [NTB, ND, 128, TB])
    dint("vspill", [NTB, ND, 128, TB])
    dint("qspill", [ND, 128, TCORE + 4], BF16)
    dint("gspill", [NTB, 8, 128])
    dint("scal_d", [1, 8])
    dint("acc_d", [128, 1])
    dint("agin", [128, 24])
    dint("agout", [NCORES, 128, 24])

    tap_t = {}
    def tap(name, shape, dt=F32):
        tap_t[name] = nc.dram_tensor(name, shape, dt, kind="ExternalOutput")
    if "kv" in taps:
        tap("tap_k", [NTB, ND, 128, TB]); tap("tap_v", [NTB, ND, 128, TB])
    if "q" in taps:
        tap("tap_q", [ND, 128, TCORE + 4], BF16)
    if "xn" in taps:
        tap("tap_xn", [NTB, ND, 128, TBH])
    if "summ" in taps:
        tap("tap_summ", [128, 24])
    if "state" in taps:
        tap("tap_state", [128, 24])
    if "o" in taps:
        tap("tap_o", [NTB, ND, 128, TB])
    if "h" in taps:
        tap("tap_h", [NTB, 128, NJ * TBH])
    if "qh" in taps:
        tap("tap_qh", [NTB, 128, NJ * TB], BF16)
    if "g" in taps:
        tap("tap_g", [NTB, 8, 128])
    if "dbg" in taps:
        tap("tap_dbg", [16, 128, 520])
    if "sx" in taps:
        tap("tap_sx", [NTB, 4, 128])
        tap("tap_isx", [NTB, 128, 4])

    sb = lambda n, s, d=F32: es.enter_context(nc.sbuf_tensor(n, s, d))
    ident = sb("ident", [128, 128])
    iota = sb("iota", [128, TB])
    sel = sb("sel", [4, TB])
    ones128 = sb("ones128", [128, 1])
    ones1 = sb("ones1", [1, 128])
    ones2 = sb("ones2", [2, 128])
    bo_sb = sb("bo_sb", [1, D])
    masks_sb = sb("masks_sb", [128, NCORES])
    wpk = sb("wpk", [128, ND]); nwpk = sb("nwpk", [128, ND]); upk = sb("upk", [128, ND])
    w511n = sb("w511n", [128, ND]); w512n = sb("w512n", [128, ND]); w4096n = sb("w4096n", [128, ND])
    bkpk = sb("bkpk", [128, ND]); bvpk = sb("bvpk", [128, ND])
    cw0 = sb("cw0", [128, NJ]); cw1 = sb("cw1", [128, NJ]); cw2 = sb("cw2", [128, NJ])
    cbpk = sb("cbpk", [128, NJ])
    sw1c = sb("sw1c", [128, 1])
    sw2c = sb("sw2c", [128, 1])
    sw1m = sb("sw1m", [128, 1])
    sw2i127 = sb("sw2i127", [128, 1])
    Mpk = sb("Mpk", [128, ND]); Apk = sb("Apk", [128, ND]); Bpk = sb("Bpk", [128, ND])
    Spp = sb("Spp", [128, ND]); Saa = sb("Saa", [128, ND]); Sbb = sb("Sbb", [128, ND])
    ppst = sb("ppst", [128, ND]); aast = sb("aast", [128, ND]); bbst = sb("bbst", [128, ND])
    f8a = sb("f8a", [128, ND]); f8b = sb("f8b", [128, ND]); f8c = sb("f8c", [128, ND])
    f8d = sb("f8d", [128, ND])
    agpack = sb("agpack", [128, 24])
    epsc = sb("epsc", [128, 1])
    agall = sb("agall", [128, NCORES * 24])

    S = Sched(nc, es)
    dbgstage = sb("dbgstage", [128, 520])
    dbg_state = {"n": 0, "prev": None}
    def snap(ap_, width, part=128):
        if "tap_dbg" not in tap_t or dbg_state["n"] >= 16:
            return
        idx = dbg_state["n"]; dbg_state["n"] += 1
        t_c = S.do("vector", lambda e: e.tensor_copy(dbgstage[0:part, 0:width], ap_),
                   after=[dbg_state["prev"]], inc="cV")
        dbg_state["prev"] = S.dma("sync", tap_t["tap_dbg"].ap()[idx, 0:part, 0:width],
                                  dbgstage[0:part, 0:width], after=[t_c], slow=True)

    psA = es.enter_context(nc.psum_tensor("psA", [128, 4096], F32))
    def PS(col, width):
        return psA[:, col:col + width]
    def PSr(prow, col, width):
        return psA[0:prow, col:col + width] if col == 0 else bass.AP(psA, col, [[4096, prow], [1, width]])

    pro = contextlib.ExitStack()
    psb = lambda n, s, d=F32: pro.enter_context(nc.sbuf_tensor(n, s, d))
    wsf = psb("wsf", [128, HID])
    wsh = psb("wsh", [128, HID], BF16)
    wsl = psb("wsl", [128, D], BF16)
    wsi16 = psb("wsi16", [128, HID], I16)
    wsi8 = psb("wsi8", [128, HID], I8)
    tot1 = psb("tot1", [128, 1])
    acc1 = psb("acc1", [128, 1])
    acc32 = psb("acc32", [128, 32])
    scal1 = psb("scal1", [1, 8])

    S.do("gpsimd", lambda e: e.iota(iota[:, :], [[1, TB]], channel_multiplier=0,
                                    allow_small_or_imprecise_dtypes=True), inc="cG")
    S.do("gpsimd", lambda e: e.memset(ones128[:, :], 1.0), inc="cG")
    S.do("gpsimd", lambda e: e.memset(ones1[:, :], 1.0), inc="cG")
    S.do("gpsimd", lambda e: e.memset(ones2[:, :], 1.0), inc="cG")
    S.do("gpsimd", lambda e: e.memset(epsc[:, :], 1e-5), inc="cG")
    S.do("gpsimd", lambda e: e.memset(Spp[:, :], -1e30), inc="cG")
    S.do("gpsimd", lambda e: e.memset(Saa[:, :], 0.0), inc="cG")
    t_gpinit = S.do("gpsimd", lambda e: e.memset(Sbb[:, :], 0.0), inc="cG")

    S.dma("sync", ident[:, :], dr["iden"].ap())
    S.dma("sync", sel[:, :], dr["sel4"].ap())
    S.dma("sync", bo_sb[:, :], dr["bo_row"].ap())
    S.dma("sync", masks_sb[:, :], dr["masks"].ap())
    def pk_ap(name, n=ND):
        return bass.AP(dr[name], 0, [[1, 128], [128, n]])
    S.dma("sync", wpk[:, :], pk_ap("wdec"), slow=True)
    S.dma("sync", upk[:, :], pk_ap("urow"), slow=True)
    S.dma("sync", bkpk[:, :], pk_ap("bk"), slow=True)
    S.dma("sync", bvpk[:, :], pk_ap("bv"), slow=True)
    for t, tile in ((0, cw0), (1, cw1), (2, cw2)):
        S.dma("sync", tile[:, :], bass.AP(dr["convw"], t, [[3, 128], [3 * 128, NJ]]), slow=True)
    t_cst = S.dma("sync", cbpk[:, :], pk_ap("convb", NJ), slow=True)

    S.do("vector", lambda e: e.tensor_scalar_mul(nwpk[:, :], wpk[:, :], -1.0),
         after=[t_cst], inc="cV")
    S.do("vector", lambda e: e.tensor_scalar_mul(w511n[:, :], wpk[:, :], -float(TB - 1)), inc="cV")
    S.do("vector", lambda e: e.tensor_scalar_mul(w512n[:, :], wpk[:, :], -float(TB)), inc="cV")
    t_pk = S.do("vector", lambda e: e.tensor_scalar_mul(w4096n[:, :], wpk[:, :], -float(TCORE)),
                inc="cV")

    prev_st = []
    for (w_in, hi_nm, lo_nm) in ((wk, "wkh_s", "wkl_s"), (wv, "wvh_s", "wvl_s"),
                                 (wo, "woh_s", "wol_s")):
        for kc in range(ND):
            sl = slice(kc * 128, (kc + 1) * 128)
            t_l = S.dma("sync", wsf[:, :D], w_in.ap()[sl, :], after=prev_st)
            t_h = S.do("vector", lambda e: e.tensor_copy(wsh[:, :D], wsf[:, :D]),
                       after=[t_l], inc="cV")
            t_lo = S.do("vector", lambda e: e.tensor_tensor(wsl[:, :D], wsf[:, :D],
                                                            wsh[:, :D], ALU.subtract), inc="cV")
            t1 = S.dma("sync", dr[hi_nm].ap()[sl, :], wsh[:, :D], after=[t_h])
            t2 = S.dma("sync", dr[lo_nm].ap()[sl, :], wsl[:, :D], after=[t_lo])
            prev_st = [t1, t2, t_lo]

    def mean_scale(w_ap, n_sl, width, base, first_after):
        S.strict = True
        t_a = None
        for i in range(n_sl):
            t_l = S.dma("sync", wsf[:, :width], w_ap[i * 128:(i + 1) * 128, :],
                        after=first_after if i == 0 else [t_a])
            t_r = S.do("vector", lambda e, width=width: e.reduce_sum(
                tot1[:, :], wsf[:, :width], AX.X, apply_absolute_value=True),
                after=[t_l], inc="cV", tiny=True)
            if i == 0:
                t_a = S.do("vector", lambda e: e.tensor_copy(acc1[:, :], tot1[:, :]), inc="cV")
            else:
                t_a = S.do("vector", lambda e: e.tensor_tensor(acc1[:, :], acc1[:, :],
                                                               tot1[:, :], ALU.add), inc="cV")
        S.strict = True
        t_mm = S.do("tensor", lambda e: e.matmul(PSr(1, 0, 1), ones128[:, :], acc1[:, :],
                                                 start=True, stop=True), after=[t_a], inc="cP")
        t_s1 = S.do("vector", lambda e, base=base: e.tensor_scalar(
            scal1[0:1, base:base + 1], PSr(1, 0, 1), 1.0 / (D * HID), 1e-5,
            ALU.mult, ALU.max), after=[t_mm], inc="cV")
        t_s2 = S.do("vector", lambda e, base=base: e.reciprocal(
            scal1[0:1, base + 1:base + 2], scal1[0:1, base:base + 1]), inc="cV")
        S.strict = False
        return t_s2

    t_sw1 = mean_scale(w1t.ap(), ND, HID, 0, prev_st)
    t_sw2 = mean_scale(w2t.ap(), NJ, D, 2, [t_sw1])
    S.strict = True
    t_s127 = S.do("vector", lambda e: e.tensor_scalar_mul(scal1[0:1, 4:5], scal1[0:1, 2:3],
                                                          1.0 / 127.0), after=[t_sw2], inc="cV")
    t_sb1 = S.dma("sync", dr["scal_d"].ap(), scal1[0:1, 0:8], after=[t_s127])
    t_sb2 = S.dma("sync", scal1[0:1, 0:8], dr["scal_d"].ap(), after=[t_sb1])
    t_bc = S.do("tensor", lambda e: e.matmul(PS(512, 8), ones1[:, :], scal1[0:1, 0:8],
                                             start=True, stop=True), after=[t_sb2], inc="cP")
    S.do("vector", lambda e: e.tensor_copy(sw1m[:, :], PS(512, 8)[:, 0:1]), after=[t_bc], inc="cV")
    S.do("vector", lambda e: e.tensor_copy(sw1c[:, :], PS(512, 8)[:, 1:2]), inc="cV")
    S.do("vector", lambda e: e.tensor_copy(sw2c[:, :], PS(512, 8)[:, 3:4]), inc="cV")
    t_cols = S.do("vector", lambda e: e.tensor_copy(sw2i127[:, :], PS(512, 8)[:, 4:5]), inc="cV")
    S.strict = False

    t_prev = [t_cols]
    for kc in range(ND):
        t_l = S.dma("sync", wsf[:, :HID], w1t.ap()[kc * 128:(kc + 1) * 128, :], after=t_prev)
        t_m = S.do("vector", lambda e: e.tensor_scalar(wsf[:, :HID], wsf[:, :HID],
                                                       sw1c[:, 0:1], None, ALU.mult),
                   after=[t_l], inc="cV")
        S.do("vector", lambda e: e.tensor_copy(wsi16[:, :HID], wsf[:, :HID]), inc="cV")
        S.do("vector", lambda e: e.tensor_copy(wsh[:, :HID], wsi16[:, :HID]), inc="cV")
        S.do("vector", lambda e: e.tensor_scalar(wsh[:, :HID], wsh[:, :HID], -1.0, 1.0,
                                                 ALU.max, ALU.min), inc="cV")
        t_i8 = S.do("vector", lambda e: e.tensor_copy(wsi8[:, :HID], wsh[:, :HID]), inc="cV")
        src_ap = bass.AP(wsi8, 0, [[HID, 128], [128, NJ], [1, 128]])
        dst_ap = bass.AP(dr["w1q_s"], kc * 128 * 128, [[128, 128], [ND * 128 * 128, NJ], [1, 128]])
        t_prev = [S.dma("sync", dst_ap, src_ap, after=[t_i8])]
    for j in range(NJ):
        t_l = S.dma("sync", wsf[:, :D], w2t.ap()[j * 128:(j + 1) * 128, :], after=t_prev)
        t_m = S.do("vector", lambda e: e.tensor_scalar(wsf[:, :D], wsf[:, :D],
                                                       sw2c[:, 0:1], None, ALU.mult),
                   after=[t_l], inc="cV")
        S.do("vector", lambda e: e.tensor_copy(wsi16[:, :D], wsf[:, :D]), inc="cV")
        S.do("vector", lambda e: e.tensor_copy(wsh[:, :D], wsi16[:, :D]), inc="cV")
        S.do("vector", lambda e: e.tensor_scalar(wsh[:, :D], wsh[:, :D], -1.0, 1.0,
                                                 ALU.max, ALU.min), inc="cV")
        t_i8 = S.do("vector", lambda e: e.tensor_copy(wsi8[:, :D], wsh[:, :D]), inc="cV")
        t_prev = [S.dma("sync", dr["w2q_s"].ap()[j, :, :], wsi8[:, :D], after=[t_i8])]

    S.barrier()
    pro.close()

    p1 = contextlib.ExitStack()
    p1sb = lambda n, s, d=F32: p1.enter_context(nc.sbuf_tensor(n, s, d))
    xf = [p1sb(f"xf{k}", [128, TBH]) for k in range(ND)]
    xh = [p1sb(f"xh{k}", [128, TBH], BF16) for k in range(ND)]
    xl = [p1sb(f"xl{k}", [128, TBH], BF16) for k in range(ND)]
    qbt = p1sb("qbt", [128, TBH], BF16)
    q16t = p1sb("q16t", [128, TBH], I16)
    qf32 = p1sb("qf32", [128, TBH])
    sqt = p1sb("sqt", [128, TBH])
    sqt2 = p1sb("sqt2", [128, TBH])
    rowm = p1sb("rowm", [1, TBH]); rowq = p1sb("rowq", [1, TBH])
    rowt = p1sb("rowt", [1, TBH])
    rowpk = p1sb("rowpk", [1, 2 * TBH])
    mxt = p1sb("mxt", [128, TBH])
    mnt = p1sb("mnt", [128, TBH])
    mxtm = p1sb("mxtm", [128, 4])
    scr4 = p1sb("scr4", [128, 4])
    gtm = p1sb("gtm", [128, 4])
    sxro = p1sb("sxro", [4, 128])
    grow = p1sb("grow", [4, 128])
    halo2 = p1sb("halo2", [2, 128])
    hv2 = p1sb("hv2", [2, 4])
    hdiag = p1sb("hdiag", [2, 2])
    tpsb = p1sb("tpsb", [128, TB])
    kvb = [p1sb(f"kvb{x}", [128, TB]) for x in range(2)]
    hsc = p1sb("hsc", [128, TB])
    negM = p1sb("negM", [128, 1])
    wkhb = [p1sb(f"wkhb{k}", [128, D], BF16) for k in range(ND)]
    wklb = [p1sb(f"wklb{k}", [128, D], BF16) for k in range(ND)]
    wvhb = [p1sb(f"wvhb{k}", [128, D], BF16) for k in range(ND)]
    wvlb = [p1sb(f"wvlb{k}", [128, D], BF16) for k in range(ND)]

    ST_X, ST_Q, ST_H = 0, 512, 1024
    HALO_IS, HALO_MIS, HALO_SX = 1088, 1090, 1092
    TP2 = 1152
    BC_IS, BC_MIS, BC_SX = 1536, 2048, 2560
    TPD = 3072
    KV0, KV1 = 3072, 3584

    t_wsl = []
    for k in range(ND):
        sl = slice(k * 128, (k + 1) * 128)
        t_wsl.append(S.dma("gpsimd", wkhb[k][:, :], dr["wkh_s"].ap()[sl, :]))
        t_wsl.append(S.dma("gpsimd", wklb[k][:, :], dr["wkl_s"].ap()[sl, :]))
        t_wsl.append(S.dma("gpsimd", wvhb[k][:, :], dr["wvh_s"].ap()[sl, :]))
        t_wsl.append(S.dma("gpsimd", wvlb[k][:, :], dr["wvl_s"].ap()[sl, :]))
    t_wready = t_wsl[-1]

    prevc = {}
    for tb in range(NTB):
        col0 = tb * TB
        tx = []
        for k in range(ND):
            tx.append(S.dma("sync", xf[k][:, :],
                            bass.AP(dr["x_fm"], k * 128 * (TCORE + 2) + col0,
                                    [[TCORE + 2, 128], [1, TBH]])))
        last_stats = []
        t_m4hist = [None, None]
        for k in range(ND):
            sq_k = sqt if k % 2 == 0 else sqt2
            t_s = S.do("scalar", lambda e, k=k, sq_k=sq_k: e.activation(
                sq_k[:, :], xf[k][:, :], AF.Square),
                after=[tx[k], prevc.get("sq_rd"), t_m4hist[k % 2]], inc="cA")
            S.do("tensor", lambda e, k=k: e.matmul(PSr(1, ST_X, 512), ones128[:, :],
                                                   xf[k][:, 1:513],
                                                   start=(k == 0), stop=(k == ND - 1)),
                 after=[tx[k]], inc="cP")
            S.do("tensor", lambda e, k=k: e.matmul(
                bass.AP(psA, ST_H, [[4096, 1], [1, 2]]), ones128[:, :],
                bass.AP(xf[k], 0, [[TBH, 128], [TB + 1, 2]]),
                start=(k == 0), stop=(k == ND - 1)), inc="cP")
            S.do("tensor", lambda e, k=k, sq_k=sq_k: e.matmul(
                PSr(1, ST_Q, 512), ones128[:, :], sq_k[:, 1:513],
                start=(k == 0), stop=(k == ND - 1)),
                 after=[t_s], inc="cP")
            t_m4 = S.do("tensor", lambda e, k=k, sq_k=sq_k: e.matmul(
                bass.AP(psA, 1536, [[4096, 1], [1, 2]]), ones128[:, :],
                bass.AP(sq_k, 0, [[TBH, 128], [TB + 1, 2]]),
                start=(k == 0), stop=(k == ND - 1)), inc="cP")
            t_m4hist[k % 2] = t_m4
            last_stats = [t_m4]
        S.do("vector", lambda e: e.tensor_scalar_mul(rowm[:, 1:513], PSr(1, ST_X, 512), 1.0 / D),
             after=last_stats + [prevc.get("rows_done")], inc="cV")
        S.do("vector", lambda e: e.tensor_scalar_mul(
            bass.AP(rowm, 0, [[TBH, 1], [TB + 1, 2]]),
            bass.AP(psA, ST_H, [[4096, 1], [1, 2]]), 1.0 / D), inc="cV", tiny=True)
        S.do("vector", lambda e: e.tensor_scalar_mul(rowq[:, 1:513], PSr(1, ST_Q, 512), 1.0 / D),
             inc="cV")
        S.do("vector", lambda e: e.tensor_scalar_mul(
            bass.AP(rowq, 0, [[TBH, 1], [TB + 1, 2]]),
            bass.AP(psA, 1536, [[4096, 1], [1, 2]]), 1.0 / D), inc="cV", tiny=True)
        S.do("vector", lambda e: e.tensor_tensor(rowt[:, :], rowm[:, :], rowm[:, :], ALU.mult),
             inc="cV")
        S.do("vector", lambda e: e.tensor_tensor(rowq[:, :], rowq[:, :], rowt[:, :], ALU.subtract),
             inc="cV")
        S.do("vector", lambda e: e.tensor_scalar(rowq[:, :], rowq[:, :], 1e-5, None, ALU.add),
             inc="cV")
        t_sd = S.do("scalar", lambda e: e.activation(rowt[:, :], rowq[:, :], AF.Sqrt),
                    after=[(S.sems["cV"], S.counts["cV"])], inc="cA")
        S.do("vector", lambda e: e.reciprocal(rowpk[:, 0:TBH], rowt[:, :]),
             after=[t_sd], inc="cV")
        for _nit in range(2):
            S.do("vector", lambda e: e.tensor_tensor(rowt[:, :], rowpk[:, 0:TBH],
                                                     rowpk[:, 0:TBH], ALU.mult), inc="cV")
            S.do("vector", lambda e: e.tensor_tensor(rowt[:, :], rowt[:, :], rowq[:, :],
                                                     ALU.mult), inc="cV")
            S.do("vector", lambda e: e.tensor_scalar(rowt[:, :], rowt[:, :], -0.5, 1.5,
                                                     ALU.mult, ALU.add), inc="cV")
            S.do("vector", lambda e: e.tensor_tensor(rowpk[:, 0:TBH], rowpk[:, 0:TBH],
                                                     rowt[:, :], ALU.mult), inc="cV")
        t_is = S.do("vector", lambda e: e.engine_nop(), inc="cV")
        t_mis = S.do("vector", lambda e: e.tensor_tensor(rowpk[:, TBH:2 * TBH], rowm[:, :],
                                                         rowpk[:, 0:TBH], ALU.mult), inc="cV")
        t_bb = []
        t_bb.append(S.do("tensor", lambda e: e.matmul(PS(BC_IS, 512), ones1[:, :],
                                                      rowpk[0:1, 1:513], start=True, stop=True),
                         after=[t_is, prevc.get("xn_done")], inc="cP"))
        t_bb.append(S.do("tensor", lambda e: e.matmul(
            PS(HALO_IS, 2), ones1[:, :],
            bass.AP(rowpk, 0, [[2 * TBH, 1], [TB + 1, 2]]), start=True, stop=True), inc="cP"))
        t_bb.append(S.do("tensor", lambda e: e.matmul(PS(BC_MIS, 512), ones1[:, :],
                                                      rowpk[0:1, TBH + 1:TBH + 513],
                                                      start=True, stop=True),
                         after=[t_mis], inc="cP"))
        t_bb.append(S.do("tensor", lambda e: e.matmul(
            PS(HALO_MIS, 2), ones1[:, :],
            bass.AP(rowpk, TBH, [[2 * TBH, 1], [TB + 1, 2]]), start=True, stop=True), inc="cP"))
        t_xn = []
        for k in range(ND):
            S.do("vector", lambda e, k=k: e.tensor_tensor(xf[k][:, 1:513], xf[k][:, 1:513],
                                                          PS(BC_IS, 512), ALU.mult),
                 after=t_bb, inc="cV")
            S.do("vector", lambda e, k=k: e.tensor_tensor(
                bass.AP(xf[k], 0, [[TBH, 128], [TB + 1, 2]]),
                bass.AP(xf[k], 0, [[TBH, 128], [TB + 1, 2]]), PS(HALO_IS, 2), ALU.mult),
                inc="cV", tiny=True)
            S.do("vector", lambda e, k=k: e.tensor_tensor(xf[k][:, 1:513], xf[k][:, 1:513],
                                                          PS(BC_MIS, 512), ALU.subtract), inc="cV")
            t_xn.append(S.do("vector", lambda e, k=k: e.tensor_tensor(
                bass.AP(xf[k], 0, [[TBH, 128], [TB + 1, 2]]),
                bass.AP(xf[k], 0, [[TBH, 128], [TB + 1, 2]]), PS(HALO_MIS, 2), ALU.subtract),
                inc="cV", tiny=True))
        t_xn_done = t_xn[-1]
        if "tap_xn" in tap_t:
            for k in range(ND):
                S.dma("sync", tap_t["tap_xn"].ap()[tb, k, :, :], xf[k][:, :],
                      after=[t_xn[k]])
        t_lb = []
        for k in range(ND):
            S.do("vector", lambda e, k=k: e.tensor_copy(xh[k][:, :], xf[k][:, :]),
                 after=[t_xn[k], prevc.get("kv_done")], inc="cV")
            t_lb.append(S.do("vector", lambda e, k=k: e.tensor_tensor(
                xl[k][:, :], xf[k][:, :], xh[k][:, :], ALU.subtract), inc="cV"))
        for tile_, op_ in ((mxt, ALU.max), (mnt, ALU.min)):
            S.do("vector", lambda e, tile_=tile_, op_=op_: e.tensor_tensor(
                tile_[:, :], xf[0][:, :], xf[1][:, :], op_), inc="cV")
            for kk_ in range(2, ND):
                S.do("vector", lambda e, tile_=tile_, op_=op_, kk_=kk_: e.tensor_tensor(
                    tile_[:, :], tile_[:, :], xf[kk_][:, :], op_), inc="cV")
        t_mx = S.do("vector", lambda e: e.engine_nop(), inc="cV")
        t_tr = []
        for src_t, r in [(mxt, r) for r in range(4)]:
            t_tr.append(S.do("tensor", lambda e, r=r: e.transpose(
                PS(TPD + r * 128, 128), mxt[:, 1 + r * 128:1 + (r + 1) * 128], ident[:, :]),
                after=[t_mx, prevc.get("tp_free")], inc="cP"))
        S.strict = True
        t_ev0 = S.do("vector", lambda e: e.tensor_copy(tpsb[:, :], PS(TPD, 512)),
                     after=t_tr, inc="cV")
        S.do("vector", lambda e: e.tensor_reduce(
            mxtm[:, :], bass.AP(tpsb, 0, [[TB, 128], [128, 4], [1, 128]]), AX.X, ALU.max),
            inc="cV")
        t_tr = []
        for r in range(4):
            t_tr.append(S.do("tensor", lambda e, r=r: e.transpose(
                PS(TPD + r * 128, 128), mnt[:, 1 + r * 128:1 + (r + 1) * 128], ident[:, :]),
                after=[t_ev0, dbg_state["prev"]], inc="cP"))
        t_ev = S.do("vector", lambda e: e.tensor_copy(tpsb[:, :], PS(TPD, 512)),
                    after=t_tr, inc="cV")
        S.do("vector", lambda e: e.tensor_reduce(
            gtm[:, :], bass.AP(tpsb, 0, [[TB, 128], [128, 4], [1, 128]]), AX.X, ALU.min),
            inc="cV")
        S.do("vector", lambda e: e.tensor_scalar_mul(gtm[:, :], gtm[:, :], -1.0), inc="cV")
        t_rd = S.do("vector", lambda e: e.tensor_tensor(mxtm[:, :], mxtm[:, :], gtm[:, :],
                                                        ALU.max), inc="cV")
        S.do("vector", lambda e: e.tensor_scalar(mxtm[:, :], mxtm[:, :], 1e-5, 1.0 / 127.0,
                                                 ALU.max, ALU.mult), inc="cV")
        S.do("vector", lambda e: e.tensor_scalar(gtm[:, :], mxtm[:, :], sw1m[:, 0:1], None,
                                                 ALU.mult), inc="cV")
        S.do("vector", lambda e: e.reciprocal(scr4[:, :], mxtm[:, :]), inc="cV")
        t_sx = S.do("vector", lambda e: e.tensor_copy(mxtm[:, :], scr4[:, :]), inc="cV")
        t_tr2 = S.do("tensor", lambda e: e.transpose(PSr(4, TP2, 128), mxtm[:, :], ident[:, :]),
                     after=[t_sx, dbg_state["prev"]], inc="cP")
        t_ro = S.do("vector", lambda e: e.tensor_copy(sxro[:, :], PSr(4, TP2, 128)),
                    after=[t_tr2], inc="cV")
        t_tr3 = S.do("tensor", lambda e: e.transpose(PSr(4, TP2, 128), gtm[:, :], ident[:, :]),
                     after=[t_ro], inc="cP")
        t_go = S.do("vector", lambda e: e.tensor_copy(grow[:, :], PSr(4, TP2, 128)),
                    after=[t_tr3], inc="cV")
        t_htr = S.do("tensor", lambda e: e.transpose(
            PSr(2, TP2, 128), bass.AP(mxt, 0, [[TBH, 128], [TB + 1, 2]]), ident[:, :]),
            after=[t_go], inc="cP")
        t_hev = S.do("vector", lambda e: e.tensor_copy(halo2[:, :], PSr(2, TP2, 128)),
                     after=[t_htr], inc="cV")
        t_hrd0 = S.do("vector", lambda e: e.tensor_reduce(hv2[:, 0:1], halo2[:, :], AX.X, ALU.max),
                      inc="cV")
        t_htr2 = S.do("tensor", lambda e: e.transpose(
            PSr(2, TP2, 128), bass.AP(mnt, 0, [[TBH, 128], [TB + 1, 2]]), ident[:, :]),
            after=[t_hrd0], inc="cP")
        t_hev2 = S.do("vector", lambda e: e.tensor_copy(halo2[:, :], PSr(2, TP2, 128)),
                      after=[t_htr2], inc="cV")
        S.do("vector", lambda e: e.tensor_reduce(hv2[:, 3:4], halo2[:, :], AX.X, ALU.min),
             inc="cV")
        S.do("vector", lambda e: e.tensor_scalar_mul(hv2[:, 3:4], hv2[:, 3:4], -1.0), inc="cV")
        t_hrd = S.do("vector", lambda e: e.tensor_tensor(hv2[:, 0:1], hv2[:, 0:1], hv2[:, 3:4],
                                                         ALU.max), inc="cV")
        S.do("vector", lambda e: e.tensor_scalar(hv2[:, 0:1], hv2[:, 0:1], 1e-5, 1.0 / 127.0,
                                                 ALU.max, ALU.mult), inc="cV")
        S.do("vector", lambda e: e.tensor_scalar(hv2[:, 2:3], hv2[:, 0:1], sw1m[0:2, 0:1], None,
                                                 ALU.mult), inc="cV")
        t_hsx = S.do("vector", lambda e: e.reciprocal(hv2[:, 1:2], hv2[:, 0:1]), inc="cV")
        t_dg = S.do("vector", lambda e: e.tensor_tensor(
            hdiag[:, :], ident[0:2, 0:2], bass.AP(hv2, 1, [[4, 2], [0, 2]]), ALU.mult), inc="cV")
        t_hbc = S.do("tensor", lambda e: e.matmul(PS(HALO_SX, 2), ones2[:, :], hdiag[:, :],
                                                  start=True, stop=True), after=[t_dg], inc="cP")
        if tb == 0:
            snap(xf[0][:, 506:514], 8)
            snap(mxt[:, 506:514], 8)
            snap(halo2[:, :], 128, part=2)
            snap(hv2[:, :], 4, part=2)
        t_sb = []
        for r in range(4):
            t_sb.append(S.do("tensor", lambda e, r=r: e.matmul(
                PS(BC_SX + r * 128, 128), sel[:, r * 128:(r + 1) * 128], sxro[:, :],
                start=True, stop=True), after=[t_ro], inc="cP"))
        if "tap_sx" in tap_t:
            S.dma("sync", tap_t["tap_sx"].ap()[tb], sxro[:, :], after=[t_ro])
            S.dma("sync", tap_t["tap_isx"].ap()[tb],
                  scr4[:, :] if False else mxtm[:, :], after=[t_sx])
        t_gs = S.dma("sync", dr["gspill"].ap()[tb, 0:4, :], grow[:, :], after=[t_go])
        t_gs2 = S.dma("sync", bass.AP(dr["gspill"], tb * 8 * 128 + 4 * 128, [[1, 2], [1, 1]]),
                      bass.AP(hv2, 2, [[4, 2], [1, 1]]), after=[t_dg], slow=True)
        S.strict = False
        t_qs = []
        for k in range(ND):
            S.do("vector", lambda e, k=k: e.tensor_tensor(qf32[:, 1:513], xf[k][:, 1:513],
                                                          PS(BC_SX, 512), ALU.mult),
                 after=t_sb + [t_hbc], inc="cV")
            S.do("vector", lambda e, k=k: e.tensor_tensor(
                bass.AP(qf32, 0, [[TBH, 128], [TB + 1, 2]]),
                bass.AP(xf[k], 0, [[TBH, 128], [TB + 1, 2]]), PS(HALO_SX, 2), ALU.mult),
                inc="cV", tiny=True)
            S.do("vector", lambda e: e.tensor_copy(q16t[:, :], qf32[:, :]), inc="cV")
            t_q3 = S.do("vector", lambda e: e.tensor_copy(qbt[:, :], q16t[:, :]),
                        after=[t_qs[-1] if t_qs else None], inc="cV")
            t_qs.append(S.dma("sync", bass.AP(dr["qspill"], k * 128 * (TCORE + 4) + col0,
                                              [[TCORE + 4, 128], [1, TBH]]), qbt[:, :],
                              after=[t_q3]))
        prevc["sq_rd"] = None
        prevc["rows_done"] = t_bb[-1]
        prevc["xn_done"] = t_qs[-1]
        t_spk = t_spv = None
        t_sum_prev = prevc.get("sum_prev")
        for c in range(ND):
            pc = slice(c * 128, (c + 1) * 128)
            mmk = mmv = None
            for k in range(ND):
                S.do("tensor", lambda e, k=k, pc=pc: e.matmul(
                    PS(KV0, 512), wkhb[k][:, pc], xh[k][:, 1:513],
                    start=(k == 0), stop=False),
                    after=[t_lb[k], t_wready, prevc.get("kv0_free"), t_ev], inc="cP")
                S.do("tensor", lambda e, k=k, pc=pc: e.matmul(
                    PS(KV0, 512), wkhb[k][:, pc], xl[k][:, 1:513], start=False, stop=False),
                    inc="cP")
                mmk = S.do("tensor", lambda e, k=k, pc=pc: e.matmul(
                    PS(KV0, 512), wklb[k][:, pc], xh[k][:, 1:513], start=False,
                    stop=(k == ND - 1)), inc="cP")
            for k in range(ND):
                S.do("tensor", lambda e, k=k, pc=pc: e.matmul(
                    PS(KV1, 512), wvhb[k][:, pc], xh[k][:, 1:513],
                    start=(k == 0), stop=False),
                    after=[prevc.get("kv1_free")], inc="cP")
                S.do("tensor", lambda e, k=k, pc=pc: e.matmul(
                    PS(KV1, 512), wvhb[k][:, pc], xl[k][:, 1:513], start=False, stop=False),
                    inc="cP")
                mmv = S.do("tensor", lambda e, k=k, pc=pc: e.matmul(
                    PS(KV1, 512), wvlb[k][:, pc], xh[k][:, 1:513], start=False,
                    stop=(k == ND - 1)), inc="cP")
            t_ek = S.do("scalar", lambda e, c=c: e.activation(
                kvb[0][:, :], PS(KV0, 512), AF.Identity, bias=bkpk[:, c:c + 1], scale=1.0),
                after=[mmk, t_spk, t_sum_prev], inc="cA")
            prevc["kv0_free"] = t_ek
            t_ev_ = S.do("scalar", lambda e, c=c: e.activation(
                kvb[1][:, :], PS(KV1, 512), AF.Identity, bias=bvpk[:, c:c + 1], scale=1.0),
                after=[mmv, t_spv, t_sum_prev], inc="cA")
            prevc["kv1_free"] = t_ev_
            t_spk = S.dma("sync", dr["kspill"].ap()[tb, c, :, :], kvb[0][:, :], after=[t_ek])
            t_spv = S.dma("sync", dr["vspill"].ap()[tb, c, :, :], kvb[1][:, :], after=[t_ev_])
            if "tap_k" in tap_t:
                t_spk = S.dma("sync", tap_t["tap_k"].ap()[tb, c, :, :], kvb[0][:, :], after=[t_ek])
                t_spv = S.dma("sync", tap_t["tap_v"].ap()[tb, c, :, :], kvb[1][:, :], after=[t_ev_])
            t_h = S.do("vector", lambda e, c=c: e.scalar_tensor_tensor(
                hsc[:, :], iota[:, :], wpk[:, c:c + 1], kvb[0][:, :], ALU.mult, ALU.add),
                after=[t_ek, prevc.get("hsc_free")], inc="cV")
            t_M = S.do("vector", lambda e, c=c: e.tensor_reduce(
                Mpk[:, c:c + 1], hsc[:, :], AX.X, ALU.max), inc="cV", tiny=True)
            S.strict = True
            t_nM = S.do("vector", lambda e, c=c: e.tensor_scalar_mul(
                negM[:, :], Mpk[:, c:c + 1], -1.0), after=[prevc.get("negM_free")], inc="cV")
            S.strict = False
            t_eh = S.do("scalar", lambda e, c=c: e.activation(
                hsc[:, :], hsc[:, :], AF.Exp, bias=negM[:, 0:1], scale=1.0,
                accum_out=Bpk[:, c:c + 1]), after=[t_nM], inc="cA")
            prevc["negM_free"] = t_eh
            t_sum = S.do("vector", lambda e, c=c: e.scalar_tensor_tensor(
                hsc[:, :], hsc[:, :], 1.0, kvb[1][:, :], ALU.mult, ALU.mult,
                accum_out=Apk[:, c:c + 1]), after=[t_eh, t_ev_], inc="cV")
            prevc["hsc_free"] = t_sum
            t_sum_prev = t_sum
        prevc["sum_prev"] = t_sum_prev
        prevc["kv_done"] = t_sum_prev
        prevc["tp_free"] = t_sum_prev
        S.strict = True
        S.do("vector", lambda e: e.tensor_tensor(Mpk[:, :], Mpk[:, :], w511n[:, :], ALU.add),
             after=[t_gpinit], inc="cV")
        S.do("vector", lambda e: e.tensor_tensor(f8a[:, :], Spp[:, :], w512n[:, :], ALU.add),
             inc="cV")
        S.do("vector", lambda e: e.tensor_tensor(Spp[:, :], f8a[:, :], Mpk[:, :], ALU.max), inc="cV")
        S.do("vector", lambda e: e.tensor_tensor(f8a[:, :], f8a[:, :], Spp[:, :], ALU.subtract),
             inc="cV")
        t_e1 = S.do("scalar", lambda e: e.activation(f8a[:, :], f8a[:, :], AF.Exp),
                    after=[(S.sems["cV"], S.counts["cV"])], inc="cA")
        S.do("vector", lambda e: e.tensor_tensor(f8b[:, :], Mpk[:, :], Spp[:, :], ALU.subtract),
             inc="cV")
        t_e2 = S.do("scalar", lambda e: e.activation(f8b[:, :], f8b[:, :], AF.Exp),
                    after=[(S.sems["cV"], S.counts["cV"])], inc="cA")
        S.do("vector", lambda e: e.tensor_tensor(Saa[:, :], Saa[:, :], f8a[:, :], ALU.mult),
             after=[t_e1, t_e2], inc="cV")
        S.do("vector", lambda e: e.tensor_tensor(Sbb[:, :], Sbb[:, :], f8a[:, :], ALU.mult), inc="cV")
        S.do("vector", lambda e: e.tensor_tensor(f8c[:, :], Apk[:, :], f8b[:, :], ALU.mult), inc="cV")
        S.do("vector", lambda e: e.tensor_tensor(Saa[:, :], Saa[:, :], f8c[:, :], ALU.add), inc="cV")
        S.do("vector", lambda e: e.tensor_tensor(f8c[:, :], Bpk[:, :], f8b[:, :], ALU.mult), inc="cV")
        t_fold = S.do("vector", lambda e: e.tensor_tensor(Sbb[:, :], Sbb[:, :], f8c[:, :], ALU.add),
                      inc="cV")
        prevc["sum_prev"] = t_fold
        S.strict = False
        S.barrier()

    if "tap_q" in tap_t:
        S.dma("sync", tap_t["tap_q"].ap(), dr["qspill"].ap(), after=[t_fold])
    S.barrier()
    p1.close()

    S.do("vector", lambda e: e.tensor_copy(agpack[:, 0:ND], Spp[:, :]), inc="cV")
    S.do("vector", lambda e: e.tensor_copy(agpack[:, ND:2 * ND], Saa[:, :]), inc="cV")
    t_pkd = S.do("vector", lambda e: e.tensor_copy(agpack[:, 2 * ND:3 * ND], Sbb[:, :]), inc="cV")
    if "tap_summ" in tap_t:
        S.dma("sync", tap_t["tap_summ"].ap(), agpack[:, :], after=[t_pkd])
    t_ag1 = S.dma("gpsimd", dr["agin"].ap(), agpack[:, :], after=[t_pkd])
    t_cc = S.do("gpsimd", lambda e: e.collective_compute(
        "AllGather", ALU.bypass, replica_groups=[list(range(NCORES))],
        ins=[dr["agin"].ap().opt()], outs=[dr["agout"].ap().opt()]),
        after=[t_ag1], inc="cc")
    t_ag2 = S.dma("gpsimd", agall[:, :],
                  bass.AP(dr["agout"], 0, [[24, 128], [128 * 24, NCORES], [1, 24]]),
                  after=[t_cc])
    S.do("gpsimd", lambda e: e.memset(Spp[:, :], -1e20), after=[t_pkd], inc="cG")
    S.do("gpsimd", lambda e: e.memset(Saa[:, :], 0.0), inc="cG")
    S.do("gpsimd", lambda e: e.memset(Sbb[:, :], 0.0), inc="cG")
    t_ri = S.do("gpsimd", lambda e: e.memset(f8d[:, :], -1e30), inc="cG")
    S.strict = True
    prev = [t_ag2, t_ri]
    for r in range(NCORES):
        off = r * 24
        Tp = agall[:, off:off + ND]
        Ta = agall[:, off + ND:off + 2 * ND]
        Tb_ = agall[:, off + 2 * ND:off + 3 * ND]
        m_col = masks_sb[:, r:r + 1]
        m_b = bass.AP(masks_sb, r, [[NCORES, 128], [0, ND]]).bitcast(mybir.dt.int32)
        S.do("vector", lambda e, Tp=Tp, m_b=m_b: e.select(
            f8b[:, :], m_b, Tp, f8d[:, :]), after=prev, inc="cV")
        S.do("vector", lambda e, m_col=m_col: e.scalar_tensor_tensor(
            f8a[:, :], w4096n[:, :], m_col, Spp[:, :], ALU.mult, ALU.add), inc="cV")
        S.do("vector", lambda e: e.tensor_tensor(Spp[:, :], f8a[:, :], f8b[:, :], ALU.max),
             inc="cV")
        S.do("vector", lambda e: e.tensor_tensor(f8a[:, :], f8a[:, :], Spp[:, :], ALU.subtract),
             inc="cV")
        t_x1 = S.do("scalar", lambda e: e.activation(f8a[:, :], f8a[:, :], AF.Exp),
                    after=[(S.sems["cV"], S.counts["cV"])], inc="cA")
        S.do("vector", lambda e: e.tensor_tensor(f8b[:, :], f8b[:, :], Spp[:, :], ALU.subtract),
             inc="cV")
        t_x2 = S.do("scalar", lambda e: e.activation(f8b[:, :], f8b[:, :], AF.Exp),
                    after=[(S.sems["cV"], S.counts["cV"])], inc="cA")
        S.do("vector", lambda e: e.tensor_tensor(Saa[:, :], Saa[:, :], f8a[:, :], ALU.mult),
             after=[t_x1, t_x2], inc="cV")
        S.do("vector", lambda e: e.tensor_tensor(Sbb[:, :], Sbb[:, :], f8a[:, :], ALU.mult), inc="cV")
        S.do("vector", lambda e, Ta=Ta: e.tensor_tensor(f8c[:, :], Ta, f8b[:, :], ALU.mult), inc="cV")
        S.do("vector", lambda e: e.tensor_tensor(Saa[:, :], Saa[:, :], f8c[:, :], ALU.add), inc="cV")
        S.do("vector", lambda e, Tb_=Tb_: e.tensor_tensor(f8c[:, :], Tb_, f8b[:, :], ALU.mult),
             inc="cV")
        t_last = S.do("vector", lambda e: e.tensor_tensor(Sbb[:, :], Sbb[:, :], f8c[:, :], ALU.add),
                      inc="cV")
        prev = [t_last]
    S.do("vector", lambda e: e.tensor_scalar_max(ppst[:, :], Spp[:, :], -30.0), inc="cV")
    S.do("vector", lambda e: e.tensor_copy(aast[:, :], Saa[:, :]), inc="cV")
    t_st0 = S.do("vector", lambda e: e.tensor_copy(bbst[:, :], Sbb[:, :]), inc="cV")
    S.strict = False
    if "tap_state" in tap_t:
        S.do("vector", lambda e: e.tensor_copy(agpack[:, 0:ND], ppst[:, :]), inc="cV")
        S.do("vector", lambda e: e.tensor_copy(agpack[:, ND:2 * ND], aast[:, :]), inc="cV")
        tt_ = S.do("vector", lambda e: e.tensor_copy(agpack[:, 2 * ND:3 * ND], bbst[:, :]), inc="cV")
        S.dma("sync", tap_t["tap_state"].ap(), agpack[:, :], after=[tt_])
    S.barrier()

    p2 = contextlib.ExitStack()
    p2sb = lambda n, s, d=F32: p2.enter_context(nc.sbuf_tensor(n, s, d))
    kbuf = [p2sb(f"kb{c}", [128, TB]) for c in range(ND)]
    vbuf = [p2sb(f"vb{c}", [128, TB]) for c in range(ND)]
    qfb = [p2sb(f"qf{k}", [128, TBH], BF16) for k in range(ND)]
    oh = [vbuf[c].ap().bitcast(BF16)[:, 0:TB] for c in range(ND)]
    olw = [vbuf[c].ap().bitcast(BF16)[:, TB:2 * TB] for c in range(ND)]
    pp3 = p2sb("pp3", [128, TB + 1])
    aa3 = p2sb("aa3", [128, TB + 1])
    bb3 = p2sb("bb3", [128, TB + 1])
    sc1 = p2sb("sc1", [128, TB]); sc2 = p2sb("sc2", [128, TB])
    sc3 = p2sb("sc3", [128, TB]); sc4 = p2sb("sc4", [128, TB]); sc5 = p2sb("sc5", [128, TB])
    hbig = p2sb("hbig", [128, NJ * TBH])
    qhb = p2sb("qhb", [128, NJ * TB], BF16)
    mxh = p2sb("mxh", [128, TB])
    mnh = p2sb("mnh", [128, TB])
    mxtm2 = p2sb("mxtm2", [128, 4])
    scr4b = p2sb("scr4b", [128, 4])
    ishc = p2sb("ishc", [128, 4])
    shro = p2sb("shro", [4, 128])
    grow2 = p2sb("grow2", [4, 128])
    ghv = p2sb("ghv", [2, 1])
    hdiag2 = p2sb("hdiag2", [2, 2])
    w1b = [p2sb(f"w1b{x}", [128, D], BF16) for x in range(2)]
    w2b = [p2sb(f"w2b{x}", [128, D], BF16) for x in range(2)]
    wohb = [p2sb(f"woh{x}", [128, D], BF16) for x in range(2)]
    wolb = [p2sb(f"wol{x}", [128, D], BF16) for x in range(2)]
    convA = iota
    convB = p2sb("cvB", [128, TB])
    q16f = convA.ap().bitcast(I16)[:, 0:TB]
    outsb = p2sb("outsb", [128, 4 * D])
    gb_sb = p2sb("gb_sb", [128, TB])
    ghb_sb = p2sb("ghb_sb", [128, 2])

    H0, H1, HALO_H, GB, GHB = 0, 512, 1024, 1536, 1100
    TPD2, TP22, BCSH = 3072, 1152, 2560

    for tb in range(NTB):
        col0 = tb * TB
        tk, tv, tq = [], [], []
        for c in range(ND):
            tk.append(S.dma("sync", kbuf[c][:, :], dr["kspill"].ap()[tb, c, :, :],
                            after=[prevc.get(f"kfree{c}")]))
            tv.append(S.dma("sync", vbuf[c][:, :], dr["vspill"].ap()[tb, c, :, :],
                            after=[prevc.get(f"vfree{c}")]))
        for k in range(ND):
            tq.append(S.dma("sync", qfb[k][:, :],
                            bass.AP(dr["qspill"], k * 128 * (TCORE + 4) + col0,
                                    [[TCORE + 4, 128], [1, TBH]]),
                            after=[prevc.get("qfree")]))
        t_gl = S.dma("sync", grow2[:, :], dr["gspill"].ap()[tb, 0:4, :],
                     after=[prevc.get("gfree")])
        t_gl2 = S.dma("sync", ghv[0:2, 0:1], bass.AP(dr["gspill"], tb * 8 * 128 + 4 * 128,
                                                 [[1, 2], [1, 1]]),
                      after=[prevc.get("gfree")], slow=True)
        t_gb = []
        for r in range(4):
            t_gb.append(S.do("tensor", lambda e, r=r: e.matmul(
                PS(GB + r * 128, 128), sel[:, r * 128:(r + 1) * 128], grow2[:, :],
                start=True, stop=True), after=[t_gl, prevc.get("gb_free")], inc="cP"))
        S.strict = True
        t_dg2 = S.do("vector", lambda e: e.tensor_tensor(
            hdiag2[:, :], ident[0:2, 0:2], bass.AP(ghv, 0, [[1, 2], [0, 2]]), ALU.mult),
            after=[t_gl2], inc="cV")
        if tb == 0:
            snap(ghv[:, :], 1, part=2)
        S.strict = False
        t_ghb0 = S.do("tensor", lambda e: e.matmul(PS(GHB, 2), ones2[:, :], hdiag2[:, :],
                                                   start=True, stop=True),
                      after=[t_dg2, prevc.get("gb_free")], inc="cP")
        t_gbe = S.do("vector", lambda e: e.tensor_copy(gb_sb[:, :], PS(GB, 512)),
                     after=[t_gb[-1]], inc="cV")
        t_ghb = S.do("vector", lambda e: e.tensor_copy(ghb_sb[:, :], PS(GHB, 2)),
                     after=[t_ghb0], inc="cV")
        mm_hist = {}
        mm_first = {}
        t_wl_map = {}

        def gp_gate(gate):
            if gate is not None:
                S.do("gpsimd", lambda e: e.engine_nop(), after=[gate], inc="cG")

        def emit_w1_load(j, gate=None):
            gp_gate(gate if gate is not None else prevc.get("w1_free"))
            t_wl_map[j] = S.dma("gpsimd",
                                bass.AP(w1b[j % 2], 0, [[D, 128], [128, ND], [1, 128]]),
                                bass.AP(dr["w1q_s"], j * ND * 128 * 128,
                                        [[128, 128], [128 * 128, ND], [1, 128]]))

        emit_w1_load(0)
        emit_w1_load(1)
        t_hev = []
        halo_ev = {}
        t_m_state = [None]

        def emit_halo_evac(g):
            hbase = HALO_H if g % 2 == 0 else 2048
            t_he = S.do("vector", lambda e, g=g, hbase=hbase: e.tensor_tensor(
                bass.AP(hbig, 4 * g * TBH, [[NJ * TBH, 128], [TBH, 4], [TB + 1, 2]]),
                bass.AP(psA, hbase, [[4096, 128], [2, 4], [1, 2]]),
                bass.AP(ghb_sb, 0, [[2, 128], [0, 4], [1, 2]]), ALU.mult),
                after=[mm_hist[4 * g + 3], t_ghb if g == 0 else None], inc="cV")
            halo_ev[g] = t_he
            if "tap_h" in tap_t:
                halo_ev[g] = S.dma("sync",
                    bass.AP(tap_t["tap_h"], tb * 128 * NJ * TBH + 4 * g * TBH,
                            [[NJ * TBH, 128], [1, 4 * TBH]]),
                    bass.AP(hbig, 4 * g * TBH, [[NJ * TBH, 128], [1, 4 * TBH]]),
                    after=[t_he, t_hev[4 * g + 3]])

        def emit_w1_j(j):
            wbuf = w1b[j % 2]
            t_wl = t_wl_map[j]
            hps = PS(H0 if j % 2 == 0 else H1, 512)
            mm_last = None
            for k in range(ND):
                pc = slice(k * 128, (k + 1) * 128)
                t_mm0 = S.do("tensor", lambda e, k=k, hps=hps, pc=pc, wbuf=wbuf: e.matmul(
                    hps, wbuf[:, pc], qfb[k][:, 1:513],
                    start=(k == 0), stop=(k == ND - 1)),
                    after=[t_wl if k == 0 else None,
                           tq[k] if j == 0 else None,
                           t_hev[-2] if (k == 0 and len(t_hev) >= 2) else None], inc="cP")
                if k == 0:
                    mm_first[j] = t_mm0
                hbase = (HALO_H if (j // 4) % 2 == 0 else 2048) + (j % 4) * 2
                mm_last = S.do("tensor", lambda e, k=k, hbase=hbase, pc=pc, wbuf=wbuf: e.matmul(
                    PS(hbase, 2), wbuf[:, pc],
                    bass.AP(qfb[k], 0, [[TBH, 128], [TB + 1, 2]]),
                    start=(k == 0), stop=(k == ND - 1)),
                    after=[(halo_ev.get((j // 4) - 2) if j >= 8 else
                            (t_ghb if j < 4 else None)) if k == 0 else None], inc="cP")
            mm_hist[j] = mm_last
            if j + 2 < NJ:
                emit_w1_load(j + 2, gate=mm_hist[j])
            if tb == 0 and j == 0:
                snap(gb_sb[:, :], TB)
                snap(ghb_sb[:, :], 2)
                snap(wbuf[:, 0:512], 512)
                snap(qfb[0][:, :], TBH)
                snap(hps, TB)
            t_e = S.do("vector", lambda e, j=j, hps=hps: e.tensor_tensor(
                bass.AP(hbig, j * TBH + 1, [[NJ * TBH, 128], [1, TB]]),
                hps, gb_sb[:, :], ALU.mult),
                after=[mm_last, t_gbe if j == 0 else None,
                       prevc.get("h_free") if j == 0 else None,
                       dbg_state["prev"] if (tb == 0 and j == 0) else None],
                inc="cV")
            t_hev.append(t_e)

        def emit_conv(j):
            base = j * TBH
            cvA, cvB = convA, convB
            hmid = bass.AP(hbig, base + 1, [[NJ * TBH, 128], [1, TB]])
            hlef = bass.AP(hbig, base + 0, [[NJ * TBH, 128], [1, TB]])
            hrig = bass.AP(hbig, base + 2, [[NJ * TBH, 128], [1, TB]])
            t_a = S.do("scalar", lambda e, j=j, hlef=hlef, cvA=cvA: e.activation(
                cvA[:, :], hlef, AF.Identity, bias=0.0, scale=cw0[:, j:j + 1]),
                after=[t_hev[j], halo_ev[j // 4],
                       t_wkv[-1] if j == 0 else None], inc="cA")
            t_b = S.do("vector", lambda e, j=j, hmid=hmid, cvA=cvA, cvB=cvB: e.scalar_tensor_tensor(
                cvB[:, :], hmid, cw1[:, j:j + 1], cvA[:, :], ALU.mult, ALU.add),
                after=[t_a], inc="cV")
            t_c2 = S.do("vector", lambda e, j=j, hrig=hrig, cvA=cvA, cvB=cvB: e.scalar_tensor_tensor(
                cvA[:, :], hrig, cw2[:, j:j + 1], cvB[:, :], ALU.mult, ALU.add),
                after=[t_b], inc="cV")
            t_s = S.do("scalar", lambda e, j=j, hmid=hmid, cvA=cvA: e.activation(
                hmid, cvA[:, :], AF.Silu, bias=cbpk[:, j:j + 1], scale=1.0),
                after=[t_c2], inc="cA")
            if j == 0:
                S.do("vector", lambda e, hmid=hmid: e.tensor_copy(
                    mxh[:, :], hmid), after=[t_s], inc="cV")
                t_m_state[0] = S.do("vector", lambda e, hmid=hmid: e.tensor_copy(
                    mnh[:, :], hmid), inc="cV")
            else:
                S.do("vector", lambda e, hmid=hmid: e.tensor_tensor(
                    mxh[:, :], mxh[:, :], hmid, ALU.max), after=[t_s], inc="cV")
                t_m_state[0] = S.do("vector", lambda e, hmid=hmid: e.tensor_tensor(
                    mnh[:, :], mnh[:, :], hmid, ALU.min), inc="cV")

        t_wkv = []
        pv_box = [prevc]

        def emit_wkv(c):
            pv = pv_box[0]
            nw_b = bass.AP(nwpk, c, [[ND, 128], [0, TB]])
            t_c0 = S.do("vector", lambda e, c=c: e.tensor_copy(pp3[:, 0:1], ppst[:, c:c + 1]),
                        after=[tk[c], t_st0, pv.get("pp3_free")], inc="cV", tiny=True)
            t_s1 = S.do("vector", lambda e, c=c, nw_b=nw_b: e.tensor_tensor_scan(
                pp3[:, 1:TB + 1], nw_b, kbuf[c][:, :], ppst[:, c:c + 1], ALU.add, ALU.max),
                inc="cV")
            S.do("vector", lambda e, c=c: e.tensor_copy(ppst[:, c:c + 1], pp3[:, TB:TB + 1]),
                 inc="cV", tiny=True, strict_this=True)
            t_a1 = S.do("vector", lambda e, c=c: e.scalar_tensor_tensor(
                sc1[:, :], pp3[:, 0:TB], nwpk[:, c:c + 1], pp3[:, 1:TB + 1],
                ALU.add, ALU.subtract), after=[pv.get("sc1_free")], inc="cV")
            t_e1 = S.do("scalar", lambda e: e.activation(sc1[:, :], sc1[:, :], AF.Exp),
                        after=[t_a1], inc="cA")
            t_a2 = S.do("vector", lambda e, c=c: e.tensor_tensor(
                sc2[:, :], kbuf[c][:, :], pp3[:, 1:TB + 1], ALU.subtract),
                after=[pv.get("sc2_free")], inc="cV")
            t_e2 = S.do("scalar", lambda e: e.activation(sc2[:, :], sc2[:, :], AF.Exp),
                        after=[t_a2], inc="cA")
            t_bv_ = S.do("gpsimd", lambda e, c=c: e.tensor_tensor(
                sc3[:, :], sc2[:, :], vbuf[c][:, :], ALU.mult),
                after=[t_e2, tv[c], pv.get("sc3_free")], inc="cG")
            t_w1_ = S.do("vector", lambda e, c=c: e.scalar_tensor_tensor(
                sc4[:, :], kbuf[c][:, :], upk[:, c:c + 1], pp3[:, 0:TB], ALU.add, ALU.subtract),
                after=[t_e1, pv.get("sc4_free")], inc="cV")
            t_r = S.do("scalar", lambda e: e.activation(sc4[:, :], sc4[:, :], AF.Exp),
                       after=[t_w1_], inc="cA")
            t_ca = S.do("vector", lambda e, c=c: e.tensor_copy(aa3[:, 0:1], aast[:, c:c + 1]),
                        after=[pv.get("aa3_free")], inc="cV", tiny=True)
            t_sa = S.do("vector", lambda e, c=c: e.tensor_tensor_scan(
                aa3[:, 1:TB + 1], sc1[:, :], sc3[:, :], aast[:, c:c + 1], ALU.mult, ALU.add),
                after=[t_bv_, t_e1], inc="cV")
            S.do("vector", lambda e, c=c: e.tensor_copy(aast[:, c:c + 1], aa3[:, TB:TB + 1]),
                 inc="cV", tiny=True, strict_this=True)
            t_cb = S.do("vector", lambda e, c=c: e.tensor_copy(bb3[:, 0:1], bbst[:, c:c + 1]),
                        after=[pv.get("bb3_free")], inc="cV", tiny=True)
            t_sb_ = S.do("vector", lambda e, c=c: e.tensor_tensor_scan(
                bb3[:, 1:TB + 1], sc1[:, :], sc2[:, :], bbst[:, c:c + 1], ALU.mult, ALU.add),
                inc="cV")
            S.do("vector", lambda e, c=c: e.tensor_copy(bbst[:, c:c + 1], bb3[:, TB:TB + 1]),
                 inc="cV", tiny=True, strict_this=True)
            t_n1 = S.do("gpsimd", lambda e, c=c: e.tensor_tensor(
                sc3[:, :], sc4[:, :], vbuf[c][:, :], ALU.mult), after=[t_r, t_sa], inc="cG")
            pv_n2 = S.do("vector", lambda e: e.tensor_tensor(
                sc3[:, :], sc3[:, :], aa3[:, 0:TB], ALU.add), after=[t_n1], inc="cV")
            t_d1 = S.do("gpsimd", lambda e: e.tensor_tensor(
                sc5[:, :], bb3[:, 0:TB], sc4[:, :], ALU.add),
                after=[t_r, t_sb_, pv.get("sc5_free")], inc="cG")
            t_d2 = S.do("vector", lambda e: e.reciprocal(sc2[:, :], sc5[:, :]),
                        after=[t_d1], inc="cV")
            t_o = S.do("vector", lambda e, c=c: e.tensor_tensor(
                kbuf[c][:, :], sc3[:, :], sc2[:, :], ALU.mult), inc="cV")
            if "tap_o" in tap_t:
                S.dma("sync", tap_t["tap_o"].ap()[tb, c, :, :], kbuf[c][:, :], after=[t_o])
            t_oh = S.do("vector", lambda e, c=c: e.tensor_copy(oh[c], kbuf[c][:, :]),
                        inc="cV")
            t_ol = S.do("vector", lambda e, c=c: e.tensor_tensor(
                olw[c], kbuf[c][:, :], oh[c], ALU.subtract), inc="cV")
            t_wkv.append(t_ol)
            pv_box[0] = {"pp3_free": t_w1_, "sc1_free": t_sb_, "sc2_free": t_bv_,
                         "sc3_free": pv_n2, "sc4_free": t_d1, "sc5_free": t_o,
                         "aa3_free": pv_n2, "bb3_free": t_d1}

        t_wo_l1 = {}
        t_wo_l2 = {}
        t_wo_mm = []

        def emit_wo_load(k, gate=None):
            gp_gate(gate if gate is not None else prevc.get("wo_free"))
            t_wo_l1[k] = S.dma("gpsimd", wohb[k % 2][:, :],
                               dr["woh_s"].ap()[k * 128:(k + 1) * 128, :])
            t_wo_l2[k] = S.dma("gpsimd", wolb[k % 2][:, :],
                               dr["wol_s"].ap()[k * 128:(k + 1) * 128, :])

        if INTERLEAVE:
            for g in range(ND):
                for j in range(4 * g, 4 * g + 4):
                    emit_w1_j(j)
                    if j >= 4:
                        emit_conv(j - 4)
                emit_halo_evac(g)
                emit_wkv(g)
                if g >= 6:
                    emit_wo_load(g - 6)
            for j in range(NJ - 4, NJ):
                emit_conv(j)
        else:
            if P2LVL >= 2:
                for g in range(ND):
                    emit_wkv(g)
            if P2LVL >= 3:
                for g in range(ND):
                    for j in range(4 * g, 4 * g + 4):
                        emit_w1_j(j)
                    emit_halo_evac(g)
            if P2LVL >= 4:
                for j in range(NJ):
                    emit_conv(j)
            if P2LVL >= 5:
                emit_wo_load(0)
                emit_wo_load(1)
        if P2LVL < 4:
            lastphase = (t_wkv[-1] if P2LVL >= 2 else t_gbe)
            t_nop = S.do("vector", lambda e: e.engine_nop(),
                         after=[lastphase] + ([mm_hist[NJ - 1]] if P2LVL >= 3 else []), inc="cV")
            for m in range(4):
                t_od = S.dma("sync", out_t.ap()[col0 + m * 128: col0 + (m + 1) * 128, :],
                             outsb[:, m * D:(m + 1) * D], after=[t_nop])
                prevc[f"outm{m}"] = t_od
            prevc["wo_free"] = t_nop
            prevc["w2_free"] = t_nop
            prevc["w1_free"] = mm_hist[NJ - 1] if P2LVL >= 3 else t_nop
            prevc["h_free"] = t_nop
            prevc["gb_free"] = t_nop
            prevc["gfree"] = t_ghb
            prevc["qfree"] = prevc["w1_free"]
            prevc["qhb_free"] = t_nop
            for c in range(ND):
                prevc[f"kfree{c}"] = t_wkv[c] if P2LVL >= 2 else t_nop
                prevc[f"vfree{c}"] = t_wkv[c] if P2LVL >= 2 else t_nop
            S.barrier()
            continue
        t_wkv_done = t_wkv[-1]
        t_tr = []
        for r in range(4):
            t_tr.append(S.do("tensor", lambda e, r=r: e.transpose(
                PS(TPD2 + r * 128, 128), mxh[:, r * 128:(r + 1) * 128], ident[:, :]),
                after=[t_m_state[0]], inc="cP"))
        S.strict = True
        t_ev2 = S.do("vector", lambda e: e.tensor_copy(sc3[:, :], PS(TPD2, 512)),
                     after=t_tr + [t_wkv_done], inc="cV")
        S.do("vector", lambda e: e.tensor_reduce(
            mxtm2[:, :], bass.AP(sc3, 0, [[TB, 128], [128, 4], [1, 128]]), AX.X, ALU.max),
            inc="cV")
        t_tr = []
        for r in range(4):
            t_tr.append(S.do("tensor", lambda e, r=r: e.transpose(
                PS(TPD2 + r * 128, 128), mnh[:, r * 128:(r + 1) * 128], ident[:, :]),
                after=[t_ev2], inc="cP"))
        t_ev2b = S.do("vector", lambda e: e.tensor_copy(sc3[:, :], PS(TPD2, 512)),
                      after=t_tr, inc="cV")
        S.do("vector", lambda e: e.tensor_reduce(
            ishc[:, :], bass.AP(sc3, 0, [[TB, 128], [128, 4], [1, 128]]), AX.X, ALU.min),
            inc="cV")
        S.do("vector", lambda e: e.tensor_scalar_mul(ishc[:, :], ishc[:, :], -1.0), inc="cV")
        t_rd2 = S.do("vector", lambda e: e.tensor_tensor(mxtm2[:, :], mxtm2[:, :], ishc[:, :],
                                                         ALU.max), inc="cV")
        S.do("vector", lambda e: e.tensor_scalar_max(mxtm2[:, :], mxtm2[:, :], 1e-5), inc="cV")
        S.do("vector", lambda e: e.tensor_scalar(ishc[:, :], mxtm2[:, :], sw2i127[:, 0:1],
                                                 None, ALU.mult), inc="cV")
        S.do("vector", lambda e: e.tensor_scalar_mul(mxtm2[:, :], mxtm2[:, :], 1.0 / 127.0),
             inc="cV")
        t_sh2a = S.do("vector", lambda e: e.reciprocal(scr4b[:, :], mxtm2[:, :]), inc="cV")
        t_sh2 = S.do("vector", lambda e: e.tensor_copy(mxtm2[:, :], scr4b[:, :]), inc="cV")
        t_tr2 = S.do("tensor", lambda e: e.transpose(PSr(4, TP22, 128), mxtm2[:, :], ident[:, :]),
                     after=[t_sh2], inc="cP")
        t_ro2 = S.do("vector", lambda e: e.tensor_copy(shro[:, :], PSr(4, TP22, 128)),
                     after=[t_tr2], inc="cV")
        t_shb = []
        for r in range(4):
            t_shb.append(S.do("tensor", lambda e, r=r: e.matmul(
                PS(BCSH + r * 128, 128), sel[:, r * 128:(r + 1) * 128], shro[:, :],
                start=True, stop=True), after=[t_ro2], inc="cP"))
        t_cpsh = S.do("vector", lambda e: e.tensor_copy(sc3[:, :], PS(BCSH, 512)),
                      after=[t_shb[-1]], inc="cV")
        S.strict = False
        t_qh = []
        for j in range(NJ):
            base = j * TBH
            hmid = bass.AP(hbig, base + 1, [[NJ * TBH, 128], [1, TB]])
            t_q1 = S.do("vector", lambda e, hmid=hmid: e.tensor_tensor(
                q16f, hmid, sc3[:, :], ALU.mult),
                after=[prevc.get("qhb_free") if j == 0 else None], inc="cV")
            t_q2 = S.do("vector", lambda e, j=j: e.tensor_copy(
                qhb[:, j * TB:(j + 1) * TB], q16f), inc="cV")
            t_qh.append(t_q2)
        if "tap_qh" in tap_t:
            S.dma("sync", tap_t["tap_qh"].ap()[tb].opt(), qhb[:, :], after=[t_qh[-1]])
        if P2LVL < 5:
            t_nop = S.do("vector", lambda e: e.engine_nop(), after=[t_qh[-1]], inc="cV")
            for m in range(4):
                t_od = S.dma("sync", out_t.ap()[col0 + m * 128: col0 + (m + 1) * 128, :],
                             outsb[:, m * D:(m + 1) * D], after=[t_nop])
                prevc[f"outm{m}"] = t_od
            prevc["wo_free"] = t_nop
            prevc["w2_free"] = t_nop
            prevc["w1_free"] = mm_hist[NJ - 1]
            prevc["h_free"] = t_qh[-1]
            prevc["gb_free"] = t_nop
            prevc["gfree"] = t_ghb
            prevc["qfree"] = mm_hist[NJ - 1]
            prevc["qhb_free"] = t_nop
            for c in range(ND):
                prevc[f"kfree{c}"] = t_wkv[c]
                prevc[f"vfree{c}"] = t_wkv[c]
            S.barrier()
            continue
        for k in range(ND):
            t_l1, t_l2 = t_wo_l1[k], t_wo_l2[k]
            wo_first = None
            for m in range(4):
                ms = slice(m * 128, (m + 1) * 128)
                for half in range(2):
                    cps = PS(m * 1024 + half * 512, 512)
                    hs = slice(half * 512, (half + 1) * 512)
                    t_mm0 = S.do("tensor", lambda e, k=k, ms=ms, hs=hs, cps=cps: e.matmul(
                        cps, oh[k][:, ms], wohb[k % 2][:, hs],
                        start=(k == 0), stop=False),
                        after=[t_l1 if m == 0 and half == 0 else None,
                               t_l2 if m == 0 and half == 0 else None,
                               t_wkv[k] if m == 0 and half == 0 else None,
                               t_cpsh if k == 0 and m == 0 and half == 0 else None], inc="cP")
                    if m == 0 and half == 0:
                        wo_first = t_mm0
                    S.do("tensor", lambda e, k=k, ms=ms, hs=hs, cps=cps: e.matmul(
                        cps, oh[k][:, ms], wolb[k % 2][:, hs], start=False, stop=False), inc="cP")
                    t_wo_mm.append(S.do("tensor", lambda e, k=k, ms=ms, hs=hs, cps=cps: e.matmul(
                        cps, olw[k][:, ms], wohb[k % 2][:, hs], start=False, stop=False),
                        inc="cP"))
            if k + 2 < ND:
                emit_wo_load(k + 2, gate=t_wo_mm[-1])
        t_bo_mm = []
        for m in range(4):
            for half in range(2):
                cps = PS(m * 1024 + half * 512, 512)
                t_bo_mm.append(S.do("tensor", lambda e, cps=cps, half=half: e.matmul(
                    cps, ones1[:, :], bo_sb[0:1, half * 512:(half + 1) * 512],
                    start=False, stop=True), inc="cP"))
        t_att = []
        for m in range(4):
            t_x = S.dma("sync", outsb[:, m * D:(m + 1) * D],
                        x_tm.ap()[col0 + m * 128: col0 + (m + 1) * 128, :],
                        after=[prevc.get(f"outm{m}")])
            t_e = S.do("vector", lambda e, m=m: e.tensor_tensor(
                outsb[:, m * D:(m + 1) * D], bass.AP(psA, m * 1024, [[4096, 128], [1, 1024]]),
                outsb[:, m * D:(m + 1) * D], ALU.add),
                after=[t_x, t_bo_mm[2 * m + 1]], inc="cV")
            t_att.append(t_e)
        if P2LVL < 6:
            t_nop = S.do("vector", lambda e: e.engine_nop(), after=[t_att[-1]], inc="cV")
            for m in range(4):
                t_od = S.dma("sync", out_t.ap()[col0 + m * 128: col0 + (m + 1) * 128, :],
                             outsb[:, m * D:(m + 1) * D], after=[t_nop])
                prevc[f"outm{m}"] = t_od
            prevc["wo_free"] = t_wo_mm[-1]
            prevc["w2_free"] = t_nop
            prevc["w1_free"] = mm_hist[NJ - 1]
            prevc["h_free"] = t_qh[-1]
            prevc["gb_free"] = t_nop
            prevc["gfree"] = t_ghb
            prevc["qfree"] = mm_hist[NJ - 1]
            prevc["qhb_free"] = t_nop
            for c in range(ND):
                prevc[f"kfree{c}"] = t_wkv[c]
                prevc[f"vfree{c}"] = t_wo_mm[8 * c + 7]
            S.barrier()
            continue
        t_w2_mm = []
        t_wl2_map = {}

        def emit_w2_load(j, gate=None):
            gp_gate(gate if gate is not None else prevc.get("w2_free"))
            t_wl2_map[j] = S.dma("gpsimd", w2b[j % 2][:, :], dr["w2q_s"].ap()[j, :, :])

        emit_w2_load(0)
        emit_w2_load(1)
        for j in range(NJ):
            wbuf = w2b[j % 2]
            t_wl2 = t_wl2_map[j]
            w2_first = None
            for m in range(4):
                for half in range(2):
                    cps = PS(m * 1024 + half * 512, 512)
                    t_w2_mm.append(S.do("tensor", lambda e, j=j, m=m, half=half, wbuf=wbuf, cps=cps: e.matmul(
                        cps, qhb[:, j * TB + m * 128:j * TB + (m + 1) * 128],
                        wbuf[:, half * 512:(half + 1) * 512],
                        start=(j == 0), stop=(j == NJ - 1)),
                        after=[t_wl2 if m == 0 and half == 0 else None,
                               t_qh[j] if m == 0 and half == 0 else None,
                               t_att[-1] if j == 0 and m == 0 and half == 0 else None], inc="cP"))
                    if m == 0 and half == 0:
                        w2_first = t_w2_mm[-1]
            if j + 2 < NJ:
                emit_w2_load(j + 2, gate=t_w2_mm[-1])
        t_outs = []
        for m in range(4):
            for half in range(2):
                cps = PS(m * 1024 + half * 512, 512)
                t_f = S.do("vector", lambda e, m=m, half=half, cps=cps: e.scalar_tensor_tensor(
                    outsb[:, m * D + half * 512:m * D + (half + 1) * 512],
                    cps, ishc[:, m:m + 1],
                    outsb[:, m * D + half * 512:m * D + (half + 1) * 512],
                    ALU.mult, ALU.add),
                    after=[t_w2_mm[-1]], inc="cV")
                t_outs.append(t_f)
            t_od = S.dma("sync", out_t.ap()[col0 + m * 128: col0 + (m + 1) * 128, :],
                         outsb[:, m * D:(m + 1) * D], after=[t_outs[-1]])
            prevc[f"outm{m}"] = t_od
        prevc["wo_free"] = t_wo_mm[-1]
        prevc["w2_free"] = t_w2_mm[-1]
        prevc["w1_free"] = mm_hist[NJ - 1]
        prevc["h_free"] = t_qh[-1]
        prevc["gb_free"] = t_outs[-1]
        prevc["gfree"] = t_ghb
        prevc["qfree"] = mm_hist[NJ - 1]
        prevc["qhb_free"] = t_w2_mm[-1]
        for c in range(ND):
            prevc[f"kfree{c}"] = t_wkv[c]
            prevc[f"vfree{c}"] = t_wo_mm[8 * c + 7]
        S.barrier()
    p2.close()

    S.do("sync", lambda e: e.nop(), after=[(S.sems["qS"], S.counts["qS"])])

    with nc.Block() as block:
        S.emit(block)
    return nc, tap_t


def _host_prep(inputs):
    x = np.asarray(inputs["L0_attn_x"])
    maps = []
    base = dict(
        wk=np.ascontiguousarray(inputs["L0_attn_Wk"], dtype=np.float32),
        wv=np.ascontiguousarray(inputs["L0_attn_Wv"], dtype=np.float32),
        wo=np.ascontiguousarray(inputs["L0_attn_Wo"], dtype=np.float32),
        w1t=np.ascontiguousarray(np.asarray(inputs["L0_ffn_w1"]).T, dtype=np.float32),
        w2t=np.ascontiguousarray(np.asarray(inputs["L0_ffn_w2"]).T, dtype=np.float32),
        bk=np.ascontiguousarray(inputs["L0_attn_bk"], dtype=np.float32),
        bv=np.ascontiguousarray(inputs["L0_attn_bv"], dtype=np.float32),
        bo_row=np.ascontiguousarray(np.asarray(inputs["L0_attn_bo"], dtype=np.float32)[None, :]),
        wdec=np.ascontiguousarray(np.asarray(inputs["L0_attn_w_rwkv"], dtype=np.float32).reshape(-1)),
        urow=np.ascontiguousarray(np.asarray(inputs["L0_attn_u_rwkv"], dtype=np.float32).reshape(-1)),
        convw=np.ascontiguousarray(np.asarray(inputs["L0_ffn_conv_w"], dtype=np.float32).reshape(HID, 3)),
        convb=np.ascontiguousarray(inputs["L0_ffn_conv_b"], dtype=np.float32),
        iden=np.eye(128, dtype=np.float32),
    )
    sel4 = np.zeros((4, TB), dtype=np.float32)
    for r in range(4):
        sel4[r, r * 128:(r + 1) * 128] = 1.0
    base["sel4"] = sel4
    for c in range(NCORES):
        b, qtr = c // 4, c % 4
        t0 = qtr * TCORE
        xb = np.asarray(x[b], dtype=np.float32)
        fm = np.zeros((D, TCORE + 2), dtype=np.float32)
        lo, hi = t0 - 1, t0 + TCORE + 1
        s_lo, s_hi = max(lo, 0), min(hi, 16384)
        fm[:, s_lo - lo:s_hi - lo] = xb[s_lo:s_hi].T
        masks = np.zeros((128, NCORES), dtype=np.float32)
        for r in range(4 * b, 4 * b + qtr):
            masks[:, r] = 1.0
        m = dict(base)
        m["x_fm"] = np.ascontiguousarray(fm)
        m["x_tm"] = np.ascontiguousarray(xb[t0:t0 + TCORE])
        m["masks"] = masks
        maps.append(m)
    return maps


def kernel(**inputs):
    maps = _host_prep(inputs)
    nc, _ = build()
    res = run_bass_kernel_spmd(nc, maps, core_ids=list(range(NCORES)))
    out = np.empty((2, 16384, D), dtype=np.float32)
    for c in range(NCORES):
        b, qtr = c // 4, c % 4
        out[b, qtr * TCORE:(qtr + 1) * TCORE] = res.results[c]["out"]
    return out
